# version 09
# baseline (speedup 1.0000x reference)
import numpy as np

import concourse.bass as bass
import concourse.mybir as mybir
from concourse.bass_utils import run_bass_kernel_spmd

F32 = mybir.dt.float32
F16 = mybir.dt.float16
F32R = mybir.dt.float32r
BF16 = mybir.dt.bfloat16
AF = mybir.ActivationFunctionType
ALU = mybir.AluOpType

D = 2048
NCH = 16
TT = 512
NTILES = 8
DEBUG_DUMP = None
TILES_PER_SEQ = 4
NROWS = 8 * TT

PI = float(np.pi)
SQRT2 = float(np.sqrt(2.0))
LN1E8 = float(-np.log(1e-8))
EPS = 1e-5
INV_D = 1.0 / D

_MAP = 212192
BASE = (224 * 1024 - 32 - _MAP) & ~31
OFF_R = BASE + 0
OFF_A = BASE + 32768
OFF_H = BASE + 65536
OFF_S1 = BASE + 98368
OFF_S3 = BASE + 131136
OFF_SQ = BASE + 163904
OFF_ST = BASE + 163904
OFF_WS = BASE + 180288
OFF_ROWS = BASE + 196672
OFF_ONER = BASE + 204864
OFF_ONESR = BASE + 205376
OFF_COLS = BASE + 205408
OFF_B1C = BASE + 207392
OFF_CONST = BASE + 207904
OFF_IDENT = BASE + 207936
OFF_CARRY = BASE + 208448
OFF_ONES16 = BASE + 208576
OFF_NU1 = BASE + 208608
OFF_RSTD = BASE + 208928
OFF_MRS = BASE + 210976
OFF_GBF = BASE + 212000
assert OFF_GBF + 192 <= 224 * 1024

P0_PANEL = BASE + 0
P0_MODBC0 = BASE + 131136
P0_MODBC1 = BASE + 139328
P0_PW0 = BASE + 147520
P0_PW1 = BASE + 155712
P0_W2 = BASE + 163904
P0_SPILL = BASE + 180288

COL_NAMES = [
    "mu_r", "mu_k", "mu_v", "om_r", "om_k", "om_v",
    "mu2_r", "mu2_k", "om2_r", "om2_k", "up",
    "g1", "b1", "g2", "b2",
    "l3g0", "l3b0", "l4g0", "l4b0", "l3g1", "l3b1", "l4g1", "l4b1",
    "ampb", "phb", "entpb0", "entpb1", "beff0", "beff1", "bpe0", "bpe1",
]
COL_IDX = {n: i * 16 for i, n in enumerate(COL_NAMES)}

MM_BANKS = [0, 1, 2, 3, 4, 5]

HOST_COL_INPUTS = {
    "c_mu_r": "mu_r", "c_mu_k": "mu_k", "c_mu_v": "mu_v",
    "c_mu2_r": "mu2_r", "c_mu2_k": "mu2_k", "c_u": "up",
    "c_g1": "g1", "c_b1": "b1", "c_g2": "g2", "c_b2": "b2",
    "c_l3g0": "l3g0", "c_l3b0": "l3b0", "c_l4g0": "l4g0", "c_l4b0": "l4b0",
    "c_l3g1": "l3g1", "c_l3b1": "l3b1", "c_l4g1": "l4g1", "c_l4b1": "l4b1",
    "c_ampb": "ampb", "c_phb": "phb", "c_entpb0": "entpb0", "c_entpb1": "entpb1",
}


class Res:
    __slots__ = ("w", "r", "dsem")

    def __init__(self):
        self.w = None
        self.r = []
        self.dsem = None

    def need_rd(self):
        return [self.w] if self.w else []

    def need_wr(self):
        return ([self.w] if self.w else []) + self.r


class Builder:
    ENGINES = ("pe", "dv", "ac", "gp", "sy")

    def __init__(self):
        self.streams = {e: [] for e in self.ENGINES}
        self.cnt = {}
        self.seen = {}
        self.sem = {}
        self.dma_sems = []

    def dma_sem_for(self, res):
        if res.dsem is None:
            res.dsem = f"q{len(self.dma_sems)}"
            self.dma_sems.append(res.dsem)
        return res.dsem

    def op(self, eng, fn, rd=(), wr=(), sem=None, amt=1):
        if sem is None:
            sem = eng
        agg = {}
        for r in rd:
            for s, v in r.need_rd():
                agg[s] = max(agg.get(s, 0), v)
        for r in wr:
            for s, v in r.need_wr():
                agg[s] = max(agg.get(s, 0), v)
        fw = []
        for s, v in sorted(agg.items()):
            if v > self.seen.get((eng, s), 0):
                fw.append((s, v))
                self.seen[(eng, s)] = v
        self.cnt[sem] = self.cnt.get(sem, 0) + amt
        val = self.cnt[sem]
        B = self

        def thunk(h, fw=tuple(fw), fn=fn, sem=sem, val=val, amt=amt):
            for s, v in fw:
                h.wait_ge(B.sem[s], v)
            ins = fn(h)
            ins.then_inc(B.sem[sem], amt)

        self.streams[eng].append(thunk)
        tok = (sem, val)
        for r in rd:
            r.r.append(tok)
        for r in wr:
            r.w = tok
            r.r = []
        return tok

    def raw(self, eng, fn):
        self.streams[eng].append(fn)

    def barrier(self, exclude=()):
        targets = [(s, self.cnt[s]) for s in
                   ("pe", "dv", "ac", "gp", *self.dma_sems)
                   if self.cnt.get(s, 0) > 0 and s not in exclude]
        B = self
        for e in self.ENGINES:
            fw = []
            for s, v in targets:
                if v > self.seen.get((e, s), 0):
                    fw.append((s, v))
                    self.seen[(e, s)] = v

            def thunk(h, fw=tuple(fw)):
                for s, v in fw:
                    h.wait_ge(B.sem[s], v)

            self.streams[e].append(thunk)


def build_program():
    nc = bass.Bass("TRN2", target_bir_lowering=False)
    din = {}

    def dram_in(name, shape):
        din[name] = nc.dram_tensor(name, list(shape), F32, kind="ExternalInput").ap()
        return din[name]

    x_in = dram_in("x", (NROWS, D))
    out = nc.dram_tensor("out", [NROWS, D], F32, kind="ExternalOutput").ap()
    for n in ("w_amp", "w_ph", "w_rT", "w_kT", "w_vT", "w_oT", "w_crT", "w_ckT", "w_cvT"):
        dram_in(n, (NCH, 128, NCH, 128))
    for l in range(2):
        dram_in(f"w_entT{l}", (4 * D, D))
        dram_in(f"w_entp{l}", (4 * D, D))
        for p in range(2):
            dram_in(f"w_path{l}{p}", (D, D))
            dram_in(f"r_phase{l}{p}", (D,))
            dram_in(f"r_pb{l}{p}", (D,))
    for n in HOST_COL_INPUTS:
        dram_in(n, (128, 16))
    dram_in("c_entb0", (128, 64))
    dram_in("c_entb1", (128, 64))
    dram_in("ident", (128, 128))
    dram_in("ones_col", (128, 1))
    dram_in("ones_row", (1, 128))

    wf_scr = [nc.dram_tensor(f"wf{l}", [NCH, 128, NCH, 128], BF16, kind="Internal").ap()
              for l in range(2)]
    we_scr = [nc.dram_tensor(f"we{l}", [NCH, 128, NCH, 128], BF16, kind="Internal").ap()
              for l in range(2)]
    WNAMES = ("w_amp", "w_ph", "w_rT", "w_kT", "w_vT", "w_oT",
              "w_crT", "w_ckT", "w_cvT")
    wt_scr = {n: nc.dram_tensor(f"b{n}", [NCH, 128, NCH, 128], BF16,
                                kind="Internal").ap() for n in WNAMES}

    def sb(name, shape, dtype, off):
        return nc.alloc_sbuf_tensor_at(name, list(shape), dtype, offset=off)

    R = sb("R", [128, NCH, TT], F32R, OFF_R)
    A = sb("A", [128, NCH, TT], F32R, OFF_A)
    H = sb("H", [128, NCH, TT + 1], F32R, OFF_H)
    S1 = sb("S1", [128, NCH, TT], F32R, OFF_S1)
    S3 = sb("S3", [128, NCH, TT], F32R, OFF_S3)
    SQ = sb("SQ", [128, NCH, TT], F16, OFF_SQ)
    AB = sb("AB", [128, NCH, TT], BF16, OFF_A)
    SB1 = sb("SB1", [128, NCH, TT], BF16, OFF_S1)
    SB3 = sb("SB3", [128, NCH, TT], BF16, OFF_S3)
    HB = sb("HB", [128, NCH, TT + 1], BF16, OFF_H)
    WS = [sb(f"WSb{i}", [128, NCH, 128], BF16, OFF_WS + i * 4096) for i in range(4)]
    ST = sb("ST", [128, D], F32R, OFF_ST)
    ROW = {n: sb(f"row_{n}", [1, TT], dt, OFF_ROWS + i * 2048)
           for i, (n, dt) in enumerate(
               (("m", F32R), ("v", F32), ("r", F32R), ("t", F32R)))}
    ONER = sb("oner", [1, 128], F32R, OFF_ONER)
    ONESR = sb("onesr", [128, 1], F32R, OFF_ONESR)
    COLS = sb("COLS", [128, 496], F32, OFF_COLS)
    B1C = [sb(f"b1c{l}", [128, 64], F32R, OFF_B1C + l * 256) for l in range(2)]
    CONST = sb("CONST", [128, 8], F32, OFF_CONST)
    IDENT = sb("IDENT", [128, 128], F32R, OFF_IDENT)
    CARRY = [sb(f"carry{i}", [128, NCH], F32R, OFF_CARRY + i * 64) for i in range(2)]
    ONES16 = sb("ones16", [128, 2], F16, OFF_ONES16)
    NU1 = sb("NU1", [128, 80], F32, OFF_NU1)
    RSTD = sb("RSTD", [128, 512], F32, OFF_RSTD)
    MRS = sb("MRS", [128, 512], BF16, OFF_MRS)
    GBF = sb("GBF", [128, 96], BF16, OFF_GBF)

    PANEL = sb("PANEL", [128, 64, 512], F32R, P0_PANEL)
    MODBC = [sb(f"modbc{p}", [128, D], F32, (P0_MODBC0, P0_MODBC1)[p]) for p in range(2)]
    PWB = [sb(f"pwb{p}", [128, D], F32, (P0_PW0, P0_PW1)[p]) for p in range(2)]
    W2B = sb("W2B", [128, 8, 512], F32R, P0_W2)
    PROW = [sb(f"prow{i}", [1, D], F32R, P0_PANEL + i * 8192) for i in range(7)]
    SPILL = sb("SPILL", [128, D], F32, P0_SPILL)

    PS = [nc.alloc_psum_tensor(f"ps{i}", [128, 512], F32) for i in range(8)]

    b = Builder()
    rR, rA, rH, rS1, rS3, rSQ = (Res() for _ in range(6))
    rST = rSQ
    rROWS, rCOLS, rCARRY, rMISC = (Res() for _ in range(4))
    rRSTD, rMRS = Res(), Res()
    rWS = [Res() for _ in range(4)]
    rWT = Res()
    rPS = [Res() for _ in range(8)]
    rWF = [Res() for _ in range(2)]
    rWE = [Res() for _ in range(2)]
    rPANEL, rMODBC, rPWB0, rPWB1, rPROW, rSPILL = (Res() for _ in range(6))
    rW2B = [Res(), Res()]

    def colv(name, c):
        i = COL_IDX[name]
        return COLS[:, i + c:i + c + 1]

    def colap(name):
        i = COL_IDX[name]
        return COLS[:, i:i + 16]

    ONES_F32 = CONST[:, 1:2]
    ONES_F16 = ONES16[:, 0:1]
    EPS_COL = CONST[:, 0:1]
    PI4_COL = CONST[:, 4:5]

    def dma(q, out_ap, in_ap, rd=(), wr=()):
        primary = (list(wr) + list(rd))[0]
        sem = b.dma_sem_for(primary)
        return b.op(q, lambda h, o=out_ap, i=in_ap: h.dma_start(out=o, in_=i),
                    rd=rd, wr=wr, sem=sem, amt=16)

    def wsrc(ap, m):
        return ap[m]

    for name, slot in HOST_COL_INPUTS.items():
        dma("gp", colap(slot), din[name][:, :], wr=[rCOLS])
    for l in range(2):
        dma("gp", B1C[l][:, :], din[f"c_entb{l}"][:, :].bitcast(F32R), wr=[rMISC])
    dma("gp", IDENT[:, :], din["ident"][:, :].bitcast(F32R), wr=[rMISC])
    dma("gp", ONESR[:, :], din["ones_col"][:, :].bitcast(F32R), wr=[rMISC])
    dma("gp", ONER[:, :], din["ones_row"][:, :].bitcast(F32R), wr=[rMISC])
    for i, lp in enumerate(("00", "01", "10", "11")):
        dma("gp", PROW[i][0:1, :],
            din[f"r_phase{lp}"].rearrange("(o d) -> o d", o=1).bitcast(F32R),
            wr=[rPROW])
    for p in range(2):
        dma("gp", PROW[4 + p][0:1, :],
            din[f"r_pb0{p}"].rearrange("(o d) -> o d", o=1).bitcast(F32R),
            wr=[rPROW])

    b.op("dv", lambda h: h.memset(CONST[:, 0:1], EPS), wr=[rMISC])
    b.op("dv", lambda h: h.memset(CONST[:, 1:2], 1.0), wr=[rMISC])
    b.op("dv", lambda h: h.memset(ONES16[:, 0:1], 1.0), wr=[rMISC])
    b.op("dv", lambda h: h.memset(ONES16[:, 1:2].bitcast(BF16), -1.0), wr=[rMISC])
    b.op("dv", lambda h: h.memset(CONST[:, 4:5], PI / 4), wr=[rMISC])

    for src, dst in (("mu_r", "om_r"), ("mu_k", "om_k"), ("mu_v", "om_v"),
                     ("mu2_r", "om2_r"), ("mu2_k", "om2_k")):
        b.op("dv", lambda h, s=src, d=dst: h.tensor_scalar(
            colap(d), colap(s), -1.0, 1.0, ALU.mult, ALU.add),
            rd=[rCOLS], wr=[rCOLS])
    b.op("dv", lambda h: h.tensor_scalar(
        colap("up"), colap("up"), LN1E8, None, ALU.add), rd=[rCOLS], wr=[rCOLS])

    for i in range(4):
        rw = PROW[i][0:1, :]
        tmp = PROW[6][0:1, :]
        b.op("dv", lambda h, rw=rw: h.tensor_scalar(rw, rw, PI / 4, None, ALU.add),
             rd=[rPROW], wr=[rPROW])
        for _ in range(2):
            b.op("dv", lambda h, rw=rw, t=tmp: h.tensor_scalar(
                t, rw, PI, -2 * PI, ALU.is_gt, ALU.mult), rd=[rPROW], wr=[rPROW])
            b.op("dv", lambda h, rw=rw, t=tmp: h.tensor_tensor(rw, rw, t, ALU.add),
                 rd=[rPROW], wr=[rPROW])
            b.op("dv", lambda h, rw=rw, t=tmp: h.tensor_scalar(
                t, rw, -PI, 2 * PI, ALU.is_lt, ALU.mult), rd=[rPROW], wr=[rPROW])
            b.op("dv", lambda h, rw=rw, t=tmp: h.tensor_tensor(rw, rw, t, ALU.add),
                 rd=[rPROW], wr=[rPROW])
        b.op("ac", lambda h, rw=rw: h.activation(rw, rw, AF.Sin),
             rd=[rPROW], wr=[rPROW])
        b.op("dv", lambda h, rw=rw: h.tensor_scalar(rw, rw, SQRT2, None, ALU.mult),
             rd=[rPROW], wr=[rPROW])

    b.barrier()

    def row_to_col(row_ap_full, dst_cols_ap, n_chunks, bank):
        def mk(h, row=row_ap_full, n=n_chunks, bank=bank):
            ins = None
            for c in range(n):
                ins = h.matmul(PS[bank][:, c:c + 1],
                               row[0:1, c * 128:(c + 1) * 128].bitcast(F32),
                               ONER[0:1, 0:1].bitcast(F32),
                               start=True, stop=True)
            return ins
        b.op("pe", mk, rd=[rPROW, rMISC], wr=[rPS[bank]])
        b.op("dv", lambda h, d=dst_cols_ap, n=n_chunks, bank=bank:
             h.tensor_copy(d, PS[bank][:, 0:n]), rd=[rPS[bank]], wr=[rCOLS])

    for l in range(2):
        if l == 1:
            for p in range(2):
                dma("gp", PROW[4 + p][0:1, :],
                    din[f"r_pb1{p}"].rearrange("(o d) -> o d", o=1).bitcast(F32R),
                    wr=[rPROW])
        b.op("dv", lambda h, l=l: h.tensor_tensor(
            PROW[4][0:1, :], PROW[4][0:1, :], PROW[2 * l][0:1, :], ALU.mult),
            rd=[rPROW], wr=[rPROW])
        b.op("dv", lambda h, l=l: h.tensor_tensor(
            PROW[5][0:1, :], PROW[5][0:1, :], PROW[2 * l + 1][0:1, :], ALU.mult),
            rd=[rPROW], wr=[rPROW])
        b.op("dv", lambda h: h.tensor_tensor(
            PROW[4][0:1, :], PROW[4][0:1, :], PROW[5][0:1, :], ALU.add),
            rd=[rPROW], wr=[rPROW])
        row_to_col(PROW[4][0:1, :], colap(f"bpe{l}"), 16, 6)

        for p in range(2):
            for fb in range(4):
                b.op("pe", lambda h, l=l, p=p, fb=fb: h.matmul(
                    PS[p][:, :], ONER[0:1, :],
                    PROW[2 * l + p][0:1, fb * 512:(fb + 1) * 512],
                    start=True, stop=True), rd=[rPROW, rMISC], wr=[rPS[p]])
                b.op("dv", lambda h, p=p, fb=fb: h.tensor_copy(
                    MODBC[p][:, fb * 512:(fb + 1) * 512], PS[p][:, :]),
                    rd=[rPS[p]], wr=[rMODBC])
        for dch in range(16):
            dma("sy", PWB[0][:, :], din[f"w_path{l}0"][dch * 128:(dch + 1) * 128, :],
                wr=[rPWB0])
            dma("sy", PWB[1][:, :], din[f"w_path{l}1"][dch * 128:(dch + 1) * 128, :],
                wr=[rPWB1])
            b.op("dv", lambda h: h.tensor_tensor(
                PWB[0][:, :], PWB[0][:, :], MODBC[0][:, :], ALU.mult),
                rd=[rMODBC], wr=[rPWB0])
            b.op("dv", lambda h: h.tensor_tensor(
                PWB[1][:, :], PWB[1][:, :], MODBC[1][:, :], ALU.mult),
                rd=[rMODBC], wr=[rPWB1])
            b.op("dv", lambda h: h.tensor_tensor(
                PWB[0][:, :], PWB[0][:, :], PWB[1][:, :], ALU.add),
                rd=[rPWB1], wr=[rPWB0])
            dma("gp", we_scr[l][:, :, dch, :].rearrange("m p c -> p m c"),
                PWB[0][:, :].rearrange("p (m c) -> p m c", c=128),
                rd=[rPWB0], wr=[rWE[l]])

    cast_jobs = list(WNAMES)

    def emit_cast():
        if cast_jobs:
            n_ = cast_jobs.pop(0)
            dma("gp", wt_scr[n_].rearrange("m p k c -> (m p) (k c)"),
                din[n_].rearrange("m p k c -> (m p) (k c)"), wr=[rWT])

    b.barrier()

    for l in range(2):
        entT = din[f"w_entT{l}"].rearrange("(c k) m -> k c m", k=128)
        entp = din[f"w_entp{l}"].rearrange("(c k) f -> k c f", k=128)
        for dq in range(4):
            for q in range(4):
                dma("sy", PANEL[:, q * 16:(q + 1) * 16, :],
                    entT[:, q * 16:(q + 1) * 16, dq * 512:(dq + 1) * 512].bitcast(F32R),
                    wr=[rPANEL])
            for fb in range(4):
                for jb in range(16):
                    half = (jb % 2) * 4
                    dma("sy", W2B[:, half:half + 4, :],
                        entp[:, jb * 4:jb * 4 + 4,
                             fb * 512:(fb + 1) * 512].bitcast(F32R),
                        wr=[rW2B[jb % 2]])

                    def mk(h, jb=jb, half=half, dq=dq, fb=fb, l=l):
                        ins = None
                        for j4 in range(4):
                            j = jb * 4 + j4
                            rhs = W2B[:, half + j4, :]
                            for d in range(4):
                                ins = h.matmul(PS[d][:, :],
                                               PANEL[:, j, d * 128:(d + 1) * 128],
                                               rhs,
                                               start=(j == 0), stop=(j == 63))
                            if dq == 0:
                                ins = h.matmul(PS[4][0:1, :],
                                               B1C[l][:, j:j + 1],
                                               rhs,
                                               start=(j == 0), stop=(j == 63))
                        return ins
                    wr_banks = [rPS[0], rPS[1], rPS[2], rPS[3]]
                    if dq == 0:
                        wr_banks.append(rPS[4])
                    b.op("pe", mk, rd=[rPANEL, rW2B[jb % 2], rMISC], wr=wr_banks)

                def ev(h, fb=fb):
                    ins = None
                    for d in range(4):
                        ins = h.tensor_copy(SPILL[:, d * 512:(d + 1) * 512], PS[d][:, :])
                    return ins
                b.op("dv", ev, rd=[rPS[0], rPS[1], rPS[2], rPS[3]], wr=[rSPILL])
                if dq == 0:
                    b.op("dv", lambda h: h.tensor_copy(ROW["t"][0:1, :], PS[4][0:1, :]),
                         rd=[rPS[4]], wr=[rROWS])

                    def r2c(h, l=l, fb=fb):
                        ins = None
                        for c4 in range(4):
                            ins = h.matmul(PS[5][:, c4:c4 + 1],
                                           ROW["t"][0:1, c4 * 128:(c4 + 1) * 128].bitcast(F32),
                                           ONER[0:1, 0:1].bitcast(F32),
                                           start=True, stop=True)
                        return ins
                    b.op("pe", r2c, rd=[rROWS], wr=[rPS[5]])
                    b.op("dv", lambda h, l=l, fb=fb: h.tensor_copy(
                        colap(f"beff{l}")[:, fb * 4:fb * 4 + 4], PS[5][:, 0:4]),
                        rd=[rPS[5]], wr=[rCOLS])
                for mi in range(4):
                    dma("gp",
                        wf_scr[l][fb * 4 + mi, :, dq * 4:(dq + 1) * 4, :],
                        SPILL[:, :].rearrange("p (k m c) -> p m k c",
                                              k=4, c=128)[:, mi],
                        rd=[rSPILL], wr=[rWF[l]])
            emit_cast()
        b.op("dv", lambda h, l=l: h.tensor_tensor(
            colap(f"beff{l}"), colap(f"beff{l}"), colap(f"entpb{l}"), ALU.add),
            rd=[rCOLS], wr=[rCOLS])
    while cast_jobs:
        emit_cast()

    b.barrier(exclude=(b.dma_sem_for(rWT),))

    NEG1_BF = ONES16[:, 1:2].bitcast(BF16)
    for i, nm in enumerate(("l3g0", "l3b0", "l3g1", "l3b1", "l4g0", "l4b0")):
        b.op("dv", lambda h, i=i, nm=nm: h.tensor_copy(
            GBF[:, i * 16:(i + 1) * 16], colap(nm)), rd=[rCOLS], wr=[rMISC])

    def colsums(w_ap, w_res, nuidx, g_rhs, b_rhs=None, foldcol=None):
        for m in range(16):
            wb_i = m % 4
            dma("sy", WS[wb_i][:, :, :], w_ap[m], rd=[w_res], wr=[rWS[wb_i]])

            def mkcs(h, wb_i=wb_i, m=m, g_rhs=g_rhs, b_rhs=b_rhs):
                ins = None
                for k in range(16):
                    ins = h.matmul(PS[6][:, m:m + 1], WS[wb_i][:, k, :],
                                   g_rhs(k), start=(k == 0), stop=(k == 15))
                if b_rhs is not None:
                    for k in range(16):
                        ins = h.matmul(PS[7][:, m:m + 1], WS[wb_i][:, k, :],
                                       b_rhs(k), start=(k == 0), stop=(k == 15))
                return ins
            wr_b = [rPS[6]] + ([rPS[7]] if b_rhs is not None else [])
            b.op("pe", mkcs, rd=[rWS[wb_i], rMISC], wr=wr_b)
        b.op("dv", lambda h, nuidx=nuidx: h.tensor_copy(
            NU1[:, nuidx * 16:nuidx * 16 + 16], PS[6][:, 0:16]),
            rd=[rPS[6]], wr=[rMISC])
        if g_rhs is not GONES:
            b.op("dv", lambda h, nuidx=nuidx: h.tensor_scalar(
                NU1[:, nuidx * 16:nuidx * 16 + 16],
                NU1[:, nuidx * 16:nuidx * 16 + 16], -1.0, None, ALU.mult),
                rd=[rMISC], wr=[rMISC])
        if foldcol is not None:
            b.op("dv", lambda h, foldcol=foldcol: h.tensor_tensor(
                colap(foldcol), colap(foldcol), PS[7][:, 0:16], ALU.add),
                rd=[rPS[7], rCOLS], wr=[rCOLS])

    GONES = lambda k: NEG1_BF
    colsums(wt_scr["w_amp"], rWT, 0, GONES)
    colsums(wt_scr["w_ph"], rWT, 1, GONES)
    colsums(we_scr[0], rWE[0], 2, lambda k: GBF[:, 0 * 16 + k:0 * 16 + k + 1],
            lambda k: GBF[:, 1 * 16 + k:1 * 16 + k + 1], "bpe0")
    colsums(we_scr[1], rWE[1], 3, lambda k: GBF[:, 2 * 16 + k:2 * 16 + k + 1],
            lambda k: GBF[:, 3 * 16 + k:3 * 16 + k + 1], "bpe1")
    colsums(wf_scr[1], rWF[1], 4, lambda k: GBF[:, 4 * 16 + k:4 * 16 + k + 1],
            lambda k: GBF[:, 5 * 16 + k:5 * 16 + k + 1], "beff1")

    mm_rot = [0]

    def dxd(w_ap, w_res, rhs_t, rhs_res, evac, extra_rhs_res=()):
        for m in range(16):
            wb_i = m % 4
            dma("sy", WS[wb_i][:, :, :], wsrc(w_ap, m),
                rd=[w_res] if w_res else [], wr=[rWS[wb_i]])
            bank = MM_BANKS[mm_rot[0] % 6]
            mm_rot[0] += 1

            is_h = int(rhs_t.shape[2]) == TT + 1

            def mk(h, wb_i=wb_i, bank=bank, rhs_t=rhs_t, is_h=is_h):
                ins = None
                for k in range(16):
                    rhs = (rhs_t[:, k, 1:TT + 1] if is_h else rhs_t[:, k, :])
                    ins = h.matmul(PS[bank][:, :], WS[wb_i][:, k, :],
                                   rhs,
                                   start=(k == 0), stop=(k == 15))
                return ins
            b.op("pe", mk, rd=[rWS[wb_i], rhs_res, *extra_rhs_res], wr=[rPS[bank]])
            evac(m, bank)

    def emit_ln_stats(src_t, src_res):
        b.op("ac", lambda h: h.activation(SQ[:, :, :], src_t[:, :, :], AF.Square),
             rd=[src_res], wr=[rSQ])

        def mk_sum(h):
            ins = None
            for c in range(16):
                ins = h.matmul(PS[6][0:1, :], ONESR[:, 0:1],
                               src_t[:, c, :],
                               start=(c == 0), stop=(c == 15))
            return ins
        b.op("pe", mk_sum, rd=[src_res, rMISC], wr=[rPS[6]])

        def mk_sq(h):
            ins = None
            for c in range(16):
                ins = h.matmul(PS[7][0:1, :], ONES_F16, SQ[:, c, :],
                               start=(c == 0), stop=(c == 15))
            return ins
        b.op("pe", mk_sq, rd=[rSQ, rMISC], wr=[rPS[7]])

        b.op("dv", lambda h: h.tensor_scalar(ROW["m"][0:1, :], PS[6][0:1, :],
                                             INV_D, None, ALU.mult),
             rd=[rPS[6]], wr=[rROWS])
        b.op("dv", lambda h: h.tensor_scalar(ROW["v"][0:1, :], PS[7][0:1, :],
                                             INV_D, None, ALU.mult),
             rd=[rPS[7]], wr=[rROWS])
        b.op("dv", lambda h: h.tensor_tensor(ROW["t"][0:1, :], ROW["m"][0:1, :],
                                             ROW["m"][0:1, :], ALU.mult),
             rd=[rROWS], wr=[rROWS])
        b.op("dv", lambda h: h.tensor_tensor(ROW["v"][0:1, :], ROW["v"][0:1, :],
                                             ROW["t"][0:1, :], ALU.subtract),
             rd=[rROWS], wr=[rROWS])
        b.op("ac", lambda h: h.activation(ROW["v"][0:1, :], ROW["v"][0:1, :],
                                          AF.Ln, bias=EPS_COL[0:1, :]),
             rd=[rROWS, rMISC], wr=[rROWS])
        b.op("ac", lambda h: h.activation(ROW["r"][0:1, :], ROW["v"][0:1, :],
                                          AF.Exp, scale=-0.5),
             rd=[rROWS], wr=[rROWS])
        b.op("dv", lambda h: h.tensor_tensor(ROW["m"][0:1, :], ROW["m"][0:1, :],
                                             ROW["r"][0:1, :], ALU.mult),
             rd=[rROWS], wr=[rROWS])
        b.op("pe", lambda h: h.matmul(PS[6][:, :], ONER[0:1, :],
                                      ROW["r"][0:1, :],
                                      start=True, stop=True),
             rd=[rROWS, rMISC], wr=[rPS[6]])
        b.op("pe", lambda h: h.matmul(PS[7][:, :], ONER[0:1, :],
                                      ROW["m"][0:1, :],
                                      start=True, stop=True),
             rd=[rROWS, rMISC], wr=[rPS[7]])

    def emit_ln(src_t, src_res, dst_fn, dst_res, g=None, bb=None,
                via_fn=None, via_res=None):
        emit_ln_stats(src_t, src_res)
        for c in range(16):
            d_ap = dst_fn(c)
            v_ap = via_fn(c) if via_fn is not None else d_ap
            v_res = via_res if via_fn is not None else dst_res
            b.op("dv", lambda h, c=c, v=v_ap: h.tensor_tensor(
                v, src_t[:, c, :], PS[6][:, :], ALU.mult),
                rd=[src_res, rPS[6]], wr=[v_res])
            if g is None:
                b.op("dv", lambda h, c=c, v=v_ap, d=d_ap: h.tensor_tensor(
                    d, v, PS[7][:, :], ALU.subtract),
                    rd=[rPS[7], v_res], wr=[dst_res])
            else:
                b.op("dv", lambda h, c=c, v=v_ap: h.tensor_tensor(
                    v, v, PS[7][:, :], ALU.subtract),
                    rd=[rPS[7], v_res], wr=[v_res])
                b.op("dv", lambda h, c=c, v=v_ap, d=d_ap, g=g, bb=bb: h.tensor_scalar(
                    d, v, colv(g, c), colv(bb, c), ALU.mult, ALU.add),
                    rd=[rCOLS, v_res], wr=[dst_res])

    def emit_mix(dst_t, dst_res, mu, om):
        for c in range(16):
            b.op("ac", lambda h, c=c, dst_t=dst_t: h.mul(
                dst_t[:, c, :], H[:, c, 0:TT], colv(om, c)),
                 rd=[rH, rCOLS], wr=[dst_res])
            b.op("dv", lambda h, c=c, dst_t=dst_t: h.scalar_tensor_tensor(
                dst_t[:, c, :], H[:, c, 1:TT + 1], colv(mu, c), dst_t[:, c, :],
                ALU.mult, ALU.add),
                rd=[rH, rCOLS, dst_res], wr=[dst_res])

    def t_in(tile):
        for tb in range(4):
            dma("gp", ST[:, :],
                x_in[tile * TT + tb * 128:tile * TT + (tb + 1) * 128, :].bitcast(F32R),
                rd=[rST], wr=[rST])
            for cg in range(4):
                bank = 6 + (cg % 2)

                def mk(h, cg=cg, bank=bank):
                    ins = None
                    for j in range(4):
                        c = cg * 4 + j
                        ins = h.transpose(
                            PS[bank][:, j * 128:(j + 1) * 128].bitcast(F32R),
                            ST[:, c * 128:(c + 1) * 128], IDENT[:, :])
                    return ins
                b.op("pe", mk, rd=[rST, rMISC], wr=[rPS[bank]])
                b.op("dv", lambda h, cg=cg, tb=tb, bank=bank: h.tensor_copy(
                    R[:, cg * 4:cg * 4 + 4, tb * 128:(tb + 1) * 128],
                    PS[bank][:, :].rearrange("p (a z) -> p a z", z=128)),
                    rd=[rPS[bank]], wr=[rR])

    def t_out(tile):
        for tb in range(4):
            for cg in range(4):
                bank = 6 + (cg % 2)

                def mk(h, cg=cg, tb=tb, bank=bank):
                    ins = None
                    for j in range(4):
                        c = cg * 4 + j
                        ins = h.transpose(
                            PS[bank][:, j * 128:(j + 1) * 128].bitcast(F32R),
                            R[:, c, tb * 128:(tb + 1) * 128], IDENT[:, :])
                    return ins
                b.op("pe", mk, rd=[rR, rMISC], wr=[rPS[bank]])
                b.op("dv", lambda h, cg=cg, bank=bank: h.tensor_copy(
                    ST[:, cg * 512:(cg + 1) * 512], PS[bank][:, :]),
                    rd=[rPS[bank]], wr=[rST])
            dma("gp", out[tile * TT + tb * 128:tile * TT + (tb + 1) * 128, :],
                ST[:, :].bitcast(F32), rd=[rST], wr=[rST])

    def carry_in(tile, ci):
        if tile % TILES_PER_SEQ == 0:
            b.op("dv", lambda h: h.tensor_scalar(
                H[:, :, 0], COLS[:, 0:16], 0.0, None, ALU.mult),
                rd=[rCOLS], wr=[rH])
        else:
            b.op("dv", lambda h, ci=ci: h.tensor_copy(H[:, :, 0], CARRY[ci][:, :]),
                 rd=[rCARRY], wr=[rH])

    def carry_save(ci):
        b.op("dv", lambda h, ci=ci: h.tensor_copy(CARRY[ci][:, :], H[:, :, TT]),
             rd=[rH], wr=[rCARRY])

    Hfull = H[:, :, 1:TT + 1]

    ffn_hooks = {}

    def emit_ffn():
        emit_ln_stats(R, rR)
        b.op("dv", lambda h: h.tensor_copy(RSTD[:, :], PS[6][:, :]),
             rd=[rPS[6]], wr=[rRSTD])
        b.op("dv", lambda h: h.tensor_copy(MRS[:, :], PS[7][:, :]),
             rd=[rPS[7]], wr=[rMRS])
        b.op("gp", lambda h: h.tensor_copy(SB1[:, :, :], R[:, :, :]),
             rd=[rR], wr=[rS1])

        def amp_evac(m, bank):
            b.op("dv", lambda h, m=m, bank=bank: h.tensor_tensor(
                S3[:, m, :], PS[bank][:, :], RSTD[:, :], ALU.mult),
                rd=[rPS[bank], rRSTD], wr=[rS3])
            b.op("dv", lambda h, m=m: h.scalar_tensor_tensor(
                S3[:, m, :], MRS[:, :], NU1[:, m:m + 1], S3[:, m, :],
                ALU.mult, ALU.add),
                rd=[rMRS, rMISC, rS3], wr=[rS3])
            b.op("ac", lambda h, m=m: h.activation(
                S3[:, m, :], S3[:, m, :], AF.Sigmoid, bias=colv("ampb", m)),
                rd=[rCOLS, rS3], wr=[rS3])

        def ph_evac(m, bank):
            b.op("dv", lambda h, m=m, bank=bank: h.tensor_tensor(
                H[:, m, 1:TT + 1], PS[bank][:, :], RSTD[:, :], ALU.mult),
                rd=[rPS[bank], rRSTD], wr=[rH])
            b.op("dv", lambda h, m=m: h.scalar_tensor_tensor(
                H[:, m, 1:TT + 1], MRS[:, :], NU1[:, 16 + m:17 + m],
                H[:, m, 1:TT + 1], ALU.mult, ALU.add),
                rd=[rMRS, rMISC, rH], wr=[rH])
            b.op("ac", lambda h, m=m: h.activation(
                H[:, m, 1:TT + 1], H[:, m, 1:TT + 1], AF.Tanh, bias=colv("phb", m)),
                rd=[rCOLS, rH], wr=[rH])

        dxd(wt_scr["w_amp"], rWT, SB1, rS1, amp_evac)
        if "amp" in ffn_hooks:
            ffn_hooks.pop("amp")()
            return
        dxd(wt_scr["w_ph"], rWT, SB1, rS1, ph_evac)
        if "ph" in ffn_hooks:
            ffn_hooks.pop("ph")()
            return
        b.op("dv", lambda h: h.tensor_scalar(S1[:, :, :], Hfull, 0.75, -2.0,
                                             ALU.is_gt, ALU.mult),
             rd=[rH], wr=[rS1])
        b.op("dv", lambda h: h.tensor_tensor(S1[:, :, :], S1[:, :, :], Hfull, ALU.add),
             rd=[rH, rS1], wr=[rS1])
        b.op("ac", lambda h: h.activation(S1[:, :, :], S1[:, :, :], AF.Sin,
                                          bias=PI4_COL, scale=PI),
             rd=[rS1, rMISC], wr=[rS1])
        b.op("dv", lambda h: h.scalar_tensor_tensor(
            AB[:, :, :], S1[:, :, :], SQRT2, S3[:, :, :], ALU.mult, ALU.mult),
            rd=[rS1, rS3], wr=[rA])
        if "h" in ffn_hooks:
            ffn_hooks.pop("h")()
            return
        def commute_ln(gname, bname):
            emit_ln_stats(S1, rS1)
            b.op("dv", lambda h: h.tensor_copy(RSTD[:, :], PS[6][:, :]),
                 rd=[rPS[6]], wr=[rRSTD])
            b.op("dv", lambda h: h.tensor_copy(MRS[:, :], PS[7][:, :]),
                 rd=[rPS[7]], wr=[rMRS])
            for c in range(16):
                b.op("ac", lambda h, c=c, g=gname: h.mul(
                    AB[:, c, :], S1[:, c, :], colv(g, c)),
                    rd=[rS1, rCOLS], wr=[rA])
            for c in range(16):
                b.op("dv", lambda h, c=c: h.tensor_tensor(
                    H[:, c, 1:TT + 1], S1[:, c, :], RSTD[:, :], ALU.mult),
                    rd=[rS1, rRSTD], wr=[rH])
                b.op("dv", lambda h, c=c: h.tensor_tensor(
                    H[:, c, 1:TT + 1], H[:, c, 1:TT + 1], MRS[:, :], ALU.subtract),
                    rd=[rMRS, rH], wr=[rH])
                b.op("ac", lambda h, c=c, g=gname, bb=bname: h.activation(
                    H[:, c, 1:TT + 1], H[:, c, 1:TT + 1], AF.Identity,
                    bias=colv(bb, c), scale=colv(g, c)),
                    rd=[rCOLS, rH], wr=[rH])

        def commuted_evac(nuidx, biascol):
            def ev(m, bank):
                b.op("dv", lambda h, m=m, bank=bank: h.tensor_tensor(
                    S1[:, m, :], PS[bank][:, :], RSTD[:, :], ALU.mult),
                    rd=[rPS[bank], rRSTD], wr=[rS1])
                b.op("dv", lambda h, m=m: h.scalar_tensor_tensor(
                    S1[:, m, :], MRS[:, :], NU1[:, nuidx * 16 + m:nuidx * 16 + m + 1],
                    S1[:, m, :], ALU.mult, ALU.add),
                    rd=[rMRS, rMISC, rS1], wr=[rS1])
                b.op("dv", lambda h, m=m: h.scalar_tensor_tensor(
                    S1[:, m, :], S1[:, m, :], colv(biascol, m),
                    H[:, m, 1:TT + 1], ALU.add, ALU.add),
                    rd=[rCOLS, rH, rS1], wr=[rS1])
            return ev

        dxd(wf_scr[0], rWF[0], AB, rA,
            lambda m, bank: b.op("dv", lambda h, m=m, bank=bank:
                h.scalar_tensor_tensor(S1[:, m, :], PS[bank][:, :],
                                       colv("beff0", m), AB[:, m, :],
                                       ALU.add, ALU.add),
                rd=[rPS[bank], rCOLS, rA], wr=[rS1]))
        commute_ln("l3g0", "l3b0")
        dxd(we_scr[0], rWE[0], AB, rA, commuted_evac(2, "bpe0"))
        commute_ln("l4g0", "l4b0")
        dxd(wf_scr[1], rWF[1], AB, rA, commuted_evac(4, "beff1"))
        commute_ln("l3g1", "l3b1")
        dxd(we_scr[1], rWE[1], AB, rA, commuted_evac(3, "bpe1"))
        emit_ln(S1, rS1, lambda c: S3[:, c, :], rS3, g="l4g1", bb="l4b1")
        b.op("dv", lambda h: h.scalar_tensor_tensor(
            R[:, :, :], S3[:, :, :], 0.5, R[:, :, :], ALU.mult, ALU.add),
            rd=[rS3, rR], wr=[rR])

    def emit_attn(tile):
        carry_in(tile, 0)
        emit_ln(A, rA, lambda c: H[:, c, 1:TT + 1], rH, g="g1", bb="b1")
        emit_mix(SB1, rS1, "mu_r", "om_r")
        emit_mix(SB3, rS3, "mu_k", "om_k")
        dxd(wt_scr["w_rT"], rWT, SB1, rS1,
            lambda m, bank: b.op("ac", lambda h, m=m, bank=bank: h.activation(
                SQ[:, m, :], PS[bank][:, :], AF.Sigmoid),
                rd=[rPS[bank]], wr=[rSQ]))
        emit_mix(SB1, rS1, "mu_v", "om_v")
        carry_save(0)
        dxd(wt_scr["w_kT"], rWT, SB3, rS3,
            lambda m, bank: b.op("ac", lambda h, m=m, bank=bank: h.activation(
                H[:, m, 1:TT + 1], PS[bank][:, :], AF.Sigmoid, bias=colv("up", m)),
                rd=[rPS[bank], rCOLS], wr=[rH]))
        b.op("dv", lambda h: h.tensor_tensor(Hfull, Hfull, SQ[:, :, :], ALU.mult),
             rd=[rSQ, rH], wr=[rH])
        dxd(wt_scr["w_vT"], rWT, SB1, rS1,
            lambda m, bank: b.op("dv", lambda h, m=m, bank=bank: h.tensor_tensor(
                SB3[:, m, :], PS[bank][:, :], H[:, m, 1:TT + 1], ALU.mult),
                rd=[rPS[bank], rH], wr=[rS3]))
        dxd(wt_scr["w_oT"], rWT, SB3, rS3,
            lambda m, bank: b.op("dv", lambda h, m=m, bank=bank: h.tensor_tensor(
                A[:, m, :], A[:, m, :], PS[bank][:, :], ALU.add),
                rd=[rPS[bank]], wr=[rA]))
        carry_in(tile, 1)
        emit_ln(A, rA, lambda c: H[:, c, 1:TT + 1], rH, g="g2", bb="b2")
        emit_mix(SB1, rS1, "mu2_r", "om2_r")
        emit_mix(SB3, rS3, "mu2_k", "om2_k")
        dxd(wt_scr["w_crT"], rWT, SB1, rS1,
            lambda m, bank: b.op("ac", lambda h, m=m, bank=bank: h.activation(
                SQ[:, m, :], PS[bank][:, :], AF.Sigmoid),
                rd=[rPS[bank]], wr=[rSQ]))
        carry_save(1)

        def ck_evac(m, bank):
            b.op("dv", lambda h, m=m, bank=bank: h.tensor_scalar(
                HB[:, m, 1:TT + 1], PS[bank][:, :], 0.0, None, ALU.max),
                rd=[rPS[bank]], wr=[rH])
            b.op("ac", lambda h, m=m: h.activation(
                HB[:, m, 1:TT + 1], HB[:, m, 1:TT + 1], AF.Square),
                rd=[rH], wr=[rH])
        dxd(wt_scr["w_ckT"], rWT, SB3, rS3, ck_evac)
        dxd(wt_scr["w_cvT"], rWT, HB, rH,
            lambda m, bank: b.op("dv", lambda h, m=m, bank=bank: h.tensor_tensor(
                S1[:, m, :], PS[bank][:, :], SQ[:, m, :], ALU.mult),
                rd=[rPS[bank], rSQ], wr=[rS1]))
        b.op("dv", lambda h: h.tensor_tensor(A[:, :, :], A[:, :, :], S1[:, :, :],
                                             ALU.add),
             rd=[rS1], wr=[rA])
        b.op("gp", lambda h: h.tensor_tensor(R[:, :, :], R[:, :, :], A[:, :, :],
                                             ALU.add),
             rd=[rA, rR], wr=[rR])

    sheets = {"R": (R, rR), "A": (A, rA), "S1": (S1, rS1), "S3": (S3, rS3),
              "H": (H, rH), "SB1": (SB1, rS1), "AB": (AB, rA)}

    def dbg_dump(name):
        t_, r_ = sheets[name]
        nch = int(t_.shape[1])
        for c4 in range(nch // 4):
            dma("gp", out[c4 * 128:(c4 + 1) * 128, :].bitcast(t_.dtype),
                t_[:, c4 * 4:c4 * 4 + 4, 0:TT] if int(t_.shape[2]) == TT + 1
                else t_[:, c4 * 4:c4 * 4 + 4, :],
                rd=[r_], wr=[rMISC])

    stop = [False]

    def maybe_dump(stage):
        if DEBUG_DUMP and DEBUG_DUMP[1] == stage:
            dbg_dump(DEBUG_DUMP[0])
            stop[0] = True

    if DEBUG_DUMP and DEBUG_DUMP[1] in ("amp", "ph", "h"):
        def _hk():
            dbg_dump(DEBUG_DUMP[0])
            stop[0] = True
        ffn_hooks[DEBUG_DUMP[1]] = _hk

    for t in range(NTILES):
        if stop[0]:
            break
        t_in(t)
        maybe_dump("t_in")
        if stop[0]:
            break
        emit_ffn()
        maybe_dump("ffn1")
        if stop[0]:
            break
        emit_ln(R, rR, lambda c: A[:, c, :], rA)
        emit_attn(t)
        maybe_dump("attn")
        if stop[0]:
            break
        emit_ffn()
        maybe_dump("ffn2")
        if stop[0]:
            break
        t_out(t)

    final_tok = rST.w if not stop[0] else rMISC.w
    b.raw("gp", lambda h, t=final_tok: h.wait_ge(b.sem[t[0]], t[1]))

    import contextlib
    with contextlib.ExitStack() as ctx:
        for name in ("pe", "dv", "ac", "gp", "syb", *b.dma_sems):
            b.sem[name] = ctx.enter_context(nc.semaphore(name))
        block = ctx.enter_context(nc.Block())

        @block.tensor
        def _(e):
            for th in b.streams["pe"]:
                th(e)

        @block.vector
        def _(e):
            for th in b.streams["dv"]:
                th(e)

        @block.scalar
        def _(e):
            for th in b.streams["ac"]:
                th(e)

        @block.gpsimd
        def _(e):
            for th in b.streams["gp"]:
                th(e)

        @block.sync
        def _(e):
            for th in b.streams["sy"]:
                th(e)

    return nc


_NC_CACHE = None


def _col(v):
    return np.ascontiguousarray(np.asarray(v, dtype=np.float32).reshape(-1, 128).T)


def _tile_w(w):
    return np.ascontiguousarray(
        np.asarray(w, dtype=np.float32).reshape(16, 128, 16, 128).transpose(2, 1, 0, 3))


def kernel(**inp):
    global _NC_CACHE
    f32 = lambda a: np.ascontiguousarray(np.asarray(a, dtype=np.float32))
    T = lambda a: np.asarray(a, dtype=np.float32).T

    x = f32(inp["L0_attn_x"])

    common = {
        "w_amp": _tile_w(inp["L0_ffn_amp_w"]),
        "w_ph": _tile_w(inp["L0_ffn_ph_w"]),
        "w_rT": _tile_w(T(inp["L0_attn_tm_Wr"])),
        "w_kT": _tile_w(T(inp["L0_attn_tm_Wk"])),
        "w_vT": _tile_w(T(inp["L0_attn_tm_Wv"])),
        "w_oT": _tile_w(T(inp["L0_attn_tm_Wout"])),
        "w_crT": _tile_w(T(inp["L0_attn_cm_Wr"])),
        "w_ckT": _tile_w(T(inp["L0_attn_cm_Wk"])),
        "w_cvT": _tile_w(T(inp["L0_attn_cm_Wv"])),
        "ident": np.eye(128, dtype=np.float32),
        "ones_col": np.ones((128, 1), dtype=np.float32),
        "ones_row": np.ones((1, 128), dtype=np.float32),
        "c_mu_r": _col(inp["L0_attn_tm_mu_r"]),
        "c_mu_k": _col(inp["L0_attn_tm_mu_k"]),
        "c_mu_v": _col(inp["L0_attn_tm_mu_v"]),
        "c_mu2_r": _col(inp["L0_attn_cm_mu_r"]),
        "c_mu2_k": _col(inp["L0_attn_cm_mu_k"]),
        "c_u": _col(inp["L0_attn_tm_u"]),
        "c_g1": _col(inp["L0_attn_ln1_g"]),
        "c_b1": _col(inp["L0_attn_ln1_b"]),
        "c_g2": _col(inp["L0_attn_ln2_g"]),
        "c_b2": _col(inp["L0_attn_ln2_b"]),
        "c_ampb": _col(inp["L0_ffn_amp_b"]),
        "c_phb": _col(inp["L0_ffn_ph_b"]),
    }
    ent_w = np.asarray(inp["L0_ffn_ent_w"], dtype=np.float32)
    ent_pw = np.asarray(inp["L0_ffn_ent_pw"], dtype=np.float32)
    path_w = np.asarray(inp["L0_ffn_path_w"], dtype=np.float32)
    phase = np.asarray(inp["L0_ffn_phase"], dtype=np.float32)
    path_b = np.asarray(inp["L0_ffn_path_b"], dtype=np.float32)
    ent_b = np.asarray(inp["L0_ffn_ent_b"], dtype=np.float32)
    ent_pb = np.asarray(inp["L0_ffn_ent_pb"], dtype=np.float32)
    ln3_g = np.asarray(inp["L0_ffn_ln3_g"], dtype=np.float32)
    ln3_b = np.asarray(inp["L0_ffn_ln3_b"], dtype=np.float32)
    ln4_g = np.asarray(inp["L0_ffn_ln4_g"], dtype=np.float32)
    ln4_b = np.asarray(inp["L0_ffn_ln4_b"], dtype=np.float32)
    for l in range(2):
        common[f"w_entT{l}"] = np.ascontiguousarray(T(ent_w[l]))
        common[f"w_entp{l}"] = f32(ent_pw[l])
        common[f"c_entb{l}"] = _col(ent_b[l])
        common[f"c_entpb{l}"] = _col(ent_pb[l])
        common[f"c_l3g{l}"] = _col(ln3_g[l])
        common[f"c_l3b{l}"] = _col(ln3_b[l])
        common[f"c_l4g{l}"] = _col(ln4_g[l])
        common[f"c_l4b{l}"] = _col(ln4_b[l])
        for p in range(2):
            common[f"w_path{l}{p}"] = f32(path_w[l, p])
            common[f"r_phase{l}{p}"] = f32(phase[l, p])
            common[f"r_pb{l}{p}"] = f32(path_b[l, p])

    in_maps = []
    for c in range(8):
        m = dict(common)
        m["x"] = np.ascontiguousarray(x[2 * c:2 * c + 2].reshape(NROWS, D))
        in_maps.append(m)

    if _NC_CACHE is None:
        _NC_CACHE = build_program()
    res = run_bass_kernel_spmd(_NC_CACHE, in_maps, core_ids=list(range(8)))
    outs = [res.results[c]["out"].reshape(2, 2048, 2048) for c in range(8)]
    return np.concatenate(outs, axis=0)


# version 10
# speedup vs baseline: 1.0279x; 1.0279x over previous
import numpy as np

import concourse.bass as bass
import concourse.mybir as mybir
from concourse.bass_utils import run_bass_kernel_spmd

F32 = mybir.dt.float32
F16 = mybir.dt.float16
F32R = mybir.dt.float32r
BF16 = mybir.dt.bfloat16
AF = mybir.ActivationFunctionType
ALU = mybir.AluOpType

D = 2048
NCH = 16
TT = 512
NTILES = 8
DEBUG_DUMP = None
TILES_PER_SEQ = 4
NROWS = 8 * TT

PI = float(np.pi)
SQRT2 = float(np.sqrt(2.0))
LN1E8 = float(-np.log(1e-8))
EPS = 1e-5
INV_D = 1.0 / D

_MAP = 212384
BASE = (224 * 1024 - 32 - _MAP) & ~31
OFF_R = BASE + 0
OFF_A = BASE + 32768
OFF_H = BASE + 65536
OFF_S1 = BASE + 98368
OFF_S3 = BASE + 131136
OFF_SQ = BASE + 163904
OFF_ST = BASE + 163904
OFF_WS = BASE + 180288
OFF_ROWS = BASE + 196672
OFF_ONER = BASE + 204864
OFF_ONESR = BASE + 205376
OFF_COLS = BASE + 205408
OFF_B1C = BASE + 207584
OFF_CONST = BASE + 208096
OFF_IDENT = BASE + 208128
OFF_CARRY = BASE + 208640
OFF_ONES16 = BASE + 208768
OFF_NU1 = BASE + 208800
OFF_RSTD = BASE + 209120
OFF_MRS = BASE + 211168
OFF_GBF = BASE + 212192
assert OFF_GBF + 192 <= 224 * 1024

P0_PANEL = BASE + 0
P0_MODBC0 = BASE + 131136
P0_MODBC1 = BASE + 139328
P0_PW0 = BASE + 147520
P0_PW1 = BASE + 155712
P0_W2 = BASE + 163904
P0_SPILL = BASE + 180288

COL_NAMES = [
    "mu_r", "mu_k", "mu_v", "om_r", "om_k", "om_v",
    "mu2_r", "mu2_k", "om2_r", "om2_k", "up",
    "g1", "b1", "g2", "b2",
    "l3g0", "l3b0", "l4g0", "l4b0", "l3g1", "l3b1", "l4g1", "l4b1",
    "ampb", "phb", "entpb0", "entpb1", "beff0", "beff1", "bpe0", "bpe1",
    "ng30", "ng31", "ng40",
]
COL_IDX = {n: i * 16 for i, n in enumerate(COL_NAMES)}

MM_BANKS = [0, 1, 2, 3, 4, 5]

HOST_COL_INPUTS = {
    "c_mu_r": "mu_r", "c_mu_k": "mu_k", "c_mu_v": "mu_v",
    "c_mu2_r": "mu2_r", "c_mu2_k": "mu2_k", "c_u": "up",
    "c_g1": "g1", "c_b1": "b1", "c_g2": "g2", "c_b2": "b2",
    "c_l3g0": "l3g0", "c_l3b0": "l3b0", "c_l4g0": "l4g0", "c_l4b0": "l4b0",
    "c_l3g1": "l3g1", "c_l3b1": "l3b1", "c_l4g1": "l4g1", "c_l4b1": "l4b1",
    "c_ampb": "ampb", "c_phb": "phb", "c_entpb0": "entpb0", "c_entpb1": "entpb1",
}


class Res:
    __slots__ = ("w", "r", "dsem")

    def __init__(self):
        self.w = None
        self.r = []
        self.dsem = None

    def need_rd(self):
        return [self.w] if self.w else []

    def need_wr(self):
        return ([self.w] if self.w else []) + self.r


class Builder:
    ENGINES = ("pe", "dv", "ac", "gp", "sy")

    def __init__(self):
        self.streams = {e: [] for e in self.ENGINES}
        self.cnt = {}
        self.seen = {}
        self.sem = {}
        self.dma_sems = []

    def dma_sem_for(self, res):
        if res.dsem is None:
            res.dsem = f"q{len(self.dma_sems)}"
            self.dma_sems.append(res.dsem)
        return res.dsem

    def op(self, eng, fn, rd=(), wr=(), sem=None, amt=1):
        if sem is None:
            sem = eng
        agg = {}
        for r in rd:
            for s, v in r.need_rd():
                agg[s] = max(agg.get(s, 0), v)
        for r in wr:
            for s, v in r.need_wr():
                agg[s] = max(agg.get(s, 0), v)
        fw = []
        for s, v in sorted(agg.items()):
            if v > self.seen.get((eng, s), 0):
                fw.append((s, v))
                self.seen[(eng, s)] = v
        self.cnt[sem] = self.cnt.get(sem, 0) + amt
        val = self.cnt[sem]
        B = self

        def thunk(h, fw=tuple(fw), fn=fn, sem=sem, val=val, amt=amt):
            for s, v in fw:
                h.wait_ge(B.sem[s], v)
            ins = fn(h)
            ins.then_inc(B.sem[sem], amt)

        self.streams[eng].append(thunk)
        tok = (sem, val)
        for r in rd:
            r.r.append(tok)
        for r in wr:
            r.w = tok
            r.r = []
        return tok

    def raw(self, eng, fn):
        self.streams[eng].append(fn)

    def barrier(self, exclude=()):
        targets = [(s, self.cnt[s]) for s in
                   ("pe", "dv", "ac", "gp", *self.dma_sems)
                   if self.cnt.get(s, 0) > 0 and s not in exclude]
        B = self
        for e in self.ENGINES:
            fw = []
            for s, v in targets:
                if v > self.seen.get((e, s), 0):
                    fw.append((s, v))
                    self.seen[(e, s)] = v

            def thunk(h, fw=tuple(fw)):
                for s, v in fw:
                    h.wait_ge(B.sem[s], v)

            self.streams[e].append(thunk)


def build_program():
    nc = bass.Bass("TRN2", target_bir_lowering=False)
    din = {}

    def dram_in(name, shape):
        din[name] = nc.dram_tensor(name, list(shape), F32, kind="ExternalInput").ap()
        return din[name]

    x_in = dram_in("x", (NROWS, D))
    out = nc.dram_tensor("out", [NROWS, D], F32, kind="ExternalOutput").ap()
    for n in ("w_amp", "w_ph", "w_rT", "w_kT", "w_vT", "w_oT", "w_crT", "w_ckT", "w_cvT"):
        dram_in(n, (NCH, 128, NCH, 128))
    for l in range(2):
        dram_in(f"w_entT{l}", (4 * D, D))
        dram_in(f"w_entp{l}", (4 * D, D))
        for p in range(2):
            dram_in(f"w_path{l}{p}", (D, D))
            dram_in(f"r_phase{l}{p}", (D,))
            dram_in(f"r_pb{l}{p}", (D,))
    for n in HOST_COL_INPUTS:
        dram_in(n, (128, 16))
    dram_in("c_entb0", (128, 64))
    dram_in("c_entb1", (128, 64))
    dram_in("ident", (128, 128))
    dram_in("ones_col", (128, 1))
    dram_in("ones_row", (1, 128))

    wf_scr = [nc.dram_tensor(f"wf{l}", [NCH, 128, NCH, 128], BF16, kind="Internal").ap()
              for l in range(2)]
    we_scr = [nc.dram_tensor(f"we{l}", [NCH, 128, NCH, 128], BF16, kind="Internal").ap()
              for l in range(2)]
    WNAMES = ("w_amp", "w_ph", "w_rT", "w_kT", "w_vT", "w_oT",
              "w_crT", "w_ckT", "w_cvT")
    wt_scr = {n: nc.dram_tensor(f"b{n}", [NCH, 128, NCH, 128], BF16,
                                kind="Internal").ap() for n in WNAMES}

    def sb(name, shape, dtype, off):
        return nc.alloc_sbuf_tensor_at(name, list(shape), dtype, offset=off)

    R = sb("R", [128, NCH, TT], F32R, OFF_R)
    A = sb("A", [128, NCH, TT], F32R, OFF_A)
    H = sb("H", [128, NCH, TT + 1], F32R, OFF_H)
    S1 = sb("S1", [128, NCH, TT], F32R, OFF_S1)
    S3 = sb("S3", [128, NCH, TT], F32R, OFF_S3)
    SQ = sb("SQ", [128, NCH, TT], F16, OFF_SQ)
    AB = sb("AB", [128, NCH, TT], BF16, OFF_A)
    SB1 = sb("SB1", [128, NCH, TT], BF16, OFF_S1)
    SB3 = sb("SB3", [128, NCH, TT], BF16, OFF_S3)
    HB = sb("HB", [128, NCH, TT + 1], BF16, OFF_H)
    WS = [sb(f"WSb{i}", [128, NCH, 128], BF16, OFF_WS + i * 4096) for i in range(4)]
    ST = sb("ST", [128, D], F32R, OFF_ST)
    ROW = {n: sb(f"row_{n}", [1, TT], dt, OFF_ROWS + i * 2048)
           for i, (n, dt) in enumerate(
               (("m", F32R), ("v", F32), ("r", F32R), ("t", F32R)))}
    ONER = sb("oner", [1, 128], F32R, OFF_ONER)
    ONESR = sb("onesr", [128, 1], F32R, OFF_ONESR)
    COLS = sb("COLS", [128, 544], F32, OFF_COLS)
    B1C = [sb(f"b1c{l}", [128, 64], F32R, OFF_B1C + l * 256) for l in range(2)]
    CONST = sb("CONST", [128, 8], F32, OFF_CONST)
    IDENT = sb("IDENT", [128, 128], F32R, OFF_IDENT)
    CARRY = [sb(f"carry{i}", [128, NCH], F32R, OFF_CARRY + i * 64) for i in range(2)]
    ONES16 = sb("ones16", [128, 2], F16, OFF_ONES16)
    NU1 = sb("NU1", [128, 80], F32, OFF_NU1)
    RSTD = sb("RSTD", [128, 512], F32, OFF_RSTD)
    MRS = sb("MRS", [128, 512], BF16, OFF_MRS)
    GBF = sb("GBF", [128, 96], BF16, OFF_GBF)

    PANEL = sb("PANEL", [128, 64, 512], F32R, P0_PANEL)
    MODBC = [sb(f"modbc{p}", [128, D], F32, (P0_MODBC0, P0_MODBC1)[p]) for p in range(2)]
    PWB = [sb(f"pwb{p}", [128, D], F32, (P0_PW0, P0_PW1)[p]) for p in range(2)]
    W2B = sb("W2B", [128, 8, 512], F32R, P0_W2)
    PROW = [sb(f"prow{i}", [1, D], F32R, P0_PANEL + i * 8192) for i in range(7)]
    SPILL = sb("SPILL", [128, D], F32, P0_SPILL)

    PS = [nc.alloc_psum_tensor(f"ps{i}", [128, 512], F32) for i in range(8)]

    b = Builder()
    rR, rA, rH, rS1, rS3, rSQ = (Res() for _ in range(6))
    rST = rSQ
    rROWS, rCOLS, rCARRY, rMISC = (Res() for _ in range(4))
    rRSTD, rMRS = Res(), Res()
    rWS = [Res() for _ in range(4)]
    rWT = Res()
    rPS = [Res() for _ in range(8)]
    rWF = [Res() for _ in range(2)]
    rWE = [Res() for _ in range(2)]
    rPANEL, rMODBC, rPWB0, rPWB1, rPROW, rSPILL = (Res() for _ in range(6))
    rW2B = [Res(), Res()]

    def colv(name, c):
        i = COL_IDX[name]
        return COLS[:, i + c:i + c + 1]

    def colap(name):
        i = COL_IDX[name]
        return COLS[:, i:i + 16]

    ONES_F32 = CONST[:, 1:2]
    ONES_F16 = ONES16[:, 0:1]
    EPS_COL = CONST[:, 0:1]
    PI4_COL = CONST[:, 4:5]

    def dma(q, out_ap, in_ap, rd=(), wr=()):
        primary = (list(wr) + list(rd))[0]
        sem = b.dma_sem_for(primary)
        return b.op(q, lambda h, o=out_ap, i=in_ap: h.dma_start(out=o, in_=i),
                    rd=rd, wr=wr, sem=sem, amt=16)

    def wsrc(ap, m):
        return ap[m]

    for name, slot in HOST_COL_INPUTS.items():
        dma("gp", colap(slot), din[name][:, :], wr=[rCOLS])
    for l in range(2):
        dma("gp", B1C[l][:, :], din[f"c_entb{l}"][:, :].bitcast(F32R), wr=[rMISC])
    dma("gp", IDENT[:, :], din["ident"][:, :].bitcast(F32R), wr=[rMISC])
    dma("gp", ONESR[:, :], din["ones_col"][:, :].bitcast(F32R), wr=[rMISC])
    dma("gp", ONER[:, :], din["ones_row"][:, :].bitcast(F32R), wr=[rMISC])
    for i, lp in enumerate(("00", "01", "10", "11")):
        dma("gp", PROW[i][0:1, :],
            din[f"r_phase{lp}"].rearrange("(o d) -> o d", o=1).bitcast(F32R),
            wr=[rPROW])
    for p in range(2):
        dma("gp", PROW[4 + p][0:1, :],
            din[f"r_pb0{p}"].rearrange("(o d) -> o d", o=1).bitcast(F32R),
            wr=[rPROW])

    b.op("dv", lambda h: h.memset(CONST[:, 0:1], EPS), wr=[rMISC])
    b.op("dv", lambda h: h.memset(CONST[:, 1:2], 1.0), wr=[rMISC])
    b.op("dv", lambda h: h.memset(ONES16[:, 0:1], 1.0), wr=[rMISC])
    b.op("dv", lambda h: h.memset(ONES16[:, 1:2].bitcast(BF16), -1.0), wr=[rMISC])
    b.op("dv", lambda h: h.memset(CONST[:, 4:5], PI / 4), wr=[rMISC])

    for src, dst in (("mu_r", "om_r"), ("mu_k", "om_k"), ("mu_v", "om_v"),
                     ("mu2_r", "om2_r"), ("mu2_k", "om2_k")):
        b.op("dv", lambda h, s=src, d=dst: h.tensor_scalar(
            colap(d), colap(s), -1.0, 1.0, ALU.mult, ALU.add),
            rd=[rCOLS], wr=[rCOLS])
    b.op("dv", lambda h: h.tensor_scalar(
        colap("up"), colap("up"), LN1E8, None, ALU.add), rd=[rCOLS], wr=[rCOLS])
    for s_, d_ in (("l3g0", "ng30"), ("l3g1", "ng31"), ("l4g0", "ng40")):
        b.op("dv", lambda h, s_=s_, d_=d_: h.tensor_scalar(
            colap(d_), colap(s_), -1.0, None, ALU.mult), rd=[rCOLS], wr=[rCOLS])

    for i in range(4):
        rw = PROW[i][0:1, :]
        tmp = PROW[6][0:1, :]
        b.op("dv", lambda h, rw=rw: h.tensor_scalar(rw, rw, PI / 4, None, ALU.add),
             rd=[rPROW], wr=[rPROW])
        for _ in range(2):
            b.op("dv", lambda h, rw=rw, t=tmp: h.tensor_scalar(
                t, rw, PI, -2 * PI, ALU.is_gt, ALU.mult), rd=[rPROW], wr=[rPROW])
            b.op("dv", lambda h, rw=rw, t=tmp: h.tensor_tensor(rw, rw, t, ALU.add),
                 rd=[rPROW], wr=[rPROW])
            b.op("dv", lambda h, rw=rw, t=tmp: h.tensor_scalar(
                t, rw, -PI, 2 * PI, ALU.is_lt, ALU.mult), rd=[rPROW], wr=[rPROW])
            b.op("dv", lambda h, rw=rw, t=tmp: h.tensor_tensor(rw, rw, t, ALU.add),
                 rd=[rPROW], wr=[rPROW])
        b.op("ac", lambda h, rw=rw: h.activation(rw, rw, AF.Sin),
             rd=[rPROW], wr=[rPROW])
        b.op("dv", lambda h, rw=rw: h.tensor_scalar(rw, rw, SQRT2, None, ALU.mult),
             rd=[rPROW], wr=[rPROW])

    b.barrier()

    def row_to_col(row_ap_full, dst_cols_ap, n_chunks, bank):
        def mk(h, row=row_ap_full, n=n_chunks, bank=bank):
            ins = None
            for c in range(n):
                ins = h.matmul(PS[bank][:, c:c + 1],
                               row[0:1, c * 128:(c + 1) * 128].bitcast(F32),
                               ONER[0:1, 0:1].bitcast(F32),
                               start=True, stop=True)
            return ins
        b.op("pe", mk, rd=[rPROW, rMISC], wr=[rPS[bank]])
        b.op("dv", lambda h, d=dst_cols_ap, n=n_chunks, bank=bank:
             h.tensor_copy(d, PS[bank][:, 0:n]), rd=[rPS[bank]], wr=[rCOLS])

    for l in range(2):
        if l == 1:
            for p in range(2):
                dma("gp", PROW[4 + p][0:1, :],
                    din[f"r_pb1{p}"].rearrange("(o d) -> o d", o=1).bitcast(F32R),
                    wr=[rPROW])
        b.op("dv", lambda h, l=l: h.tensor_tensor(
            PROW[4][0:1, :], PROW[4][0:1, :], PROW[2 * l][0:1, :], ALU.mult),
            rd=[rPROW], wr=[rPROW])
        b.op("dv", lambda h, l=l: h.tensor_tensor(
            PROW[5][0:1, :], PROW[5][0:1, :], PROW[2 * l + 1][0:1, :], ALU.mult),
            rd=[rPROW], wr=[rPROW])
        b.op("dv", lambda h: h.tensor_tensor(
            PROW[4][0:1, :], PROW[4][0:1, :], PROW[5][0:1, :], ALU.add),
            rd=[rPROW], wr=[rPROW])
        row_to_col(PROW[4][0:1, :], colap(f"bpe{l}"), 16, 6)

        for p in range(2):
            for fb in range(4):
                b.op("pe", lambda h, l=l, p=p, fb=fb: h.matmul(
                    PS[p][:, :], ONER[0:1, :],
                    PROW[2 * l + p][0:1, fb * 512:(fb + 1) * 512],
                    start=True, stop=True), rd=[rPROW, rMISC], wr=[rPS[p]])
                b.op("dv", lambda h, p=p, fb=fb: h.tensor_copy(
                    MODBC[p][:, fb * 512:(fb + 1) * 512], PS[p][:, :]),
                    rd=[rPS[p]], wr=[rMODBC])
        for dch in range(16):
            dma("sy", PWB[0][:, :], din[f"w_path{l}0"][dch * 128:(dch + 1) * 128, :],
                wr=[rPWB0])
            dma("sy", PWB[1][:, :], din[f"w_path{l}1"][dch * 128:(dch + 1) * 128, :],
                wr=[rPWB1])
            b.op("dv", lambda h: h.tensor_tensor(
                PWB[0][:, :], PWB[0][:, :], MODBC[0][:, :], ALU.mult),
                rd=[rMODBC], wr=[rPWB0])
            b.op("dv", lambda h: h.tensor_tensor(
                PWB[1][:, :], PWB[1][:, :], MODBC[1][:, :], ALU.mult),
                rd=[rMODBC], wr=[rPWB1])
            b.op("dv", lambda h: h.tensor_tensor(
                PWB[0][:, :], PWB[0][:, :], PWB[1][:, :], ALU.add),
                rd=[rPWB1], wr=[rPWB0])
            dma("gp", we_scr[l][:, :, dch, :].rearrange("m p c -> p m c"),
                PWB[0][:, :].rearrange("p (m c) -> p m c", c=128),
                rd=[rPWB0], wr=[rWE[l]])

    cast_jobs = list(WNAMES)

    def emit_cast():
        if cast_jobs:
            n_ = cast_jobs.pop(0)
            dma("gp", wt_scr[n_].rearrange("m p k c -> (m p) (k c)"),
                din[n_].rearrange("m p k c -> (m p) (k c)"), wr=[rWT])

    b.barrier()

    for l in range(2):
        entT = din[f"w_entT{l}"].rearrange("(c k) m -> k c m", k=128)
        entp = din[f"w_entp{l}"].rearrange("(c k) f -> k c f", k=128)
        for dq in range(4):
            for q in range(4):
                dma("sy", PANEL[:, q * 16:(q + 1) * 16, :],
                    entT[:, q * 16:(q + 1) * 16, dq * 512:(dq + 1) * 512].bitcast(F32R),
                    wr=[rPANEL])
            for fb in range(4):
                for jb in range(16):
                    half = (jb % 2) * 4
                    dma("sy", W2B[:, half:half + 4, :],
                        entp[:, jb * 4:jb * 4 + 4,
                             fb * 512:(fb + 1) * 512].bitcast(F32R),
                        wr=[rW2B[jb % 2]])

                    def mk(h, jb=jb, half=half, dq=dq, fb=fb, l=l):
                        ins = None
                        for j4 in range(4):
                            j = jb * 4 + j4
                            rhs = W2B[:, half + j4, :]
                            for d in range(4):
                                ins = h.matmul(PS[d][:, :],
                                               PANEL[:, j, d * 128:(d + 1) * 128],
                                               rhs,
                                               start=(j == 0), stop=(j == 63))
                            if dq == 0:
                                ins = h.matmul(PS[4][0:1, :],
                                               B1C[l][:, j:j + 1],
                                               rhs,
                                               start=(j == 0), stop=(j == 63))
                        return ins
                    wr_banks = [rPS[0], rPS[1], rPS[2], rPS[3]]
                    if dq == 0:
                        wr_banks.append(rPS[4])
                    b.op("pe", mk, rd=[rPANEL, rW2B[jb % 2], rMISC], wr=wr_banks)

                def ev(h, fb=fb):
                    ins = None
                    for d in range(4):
                        ins = h.tensor_copy(SPILL[:, d * 512:(d + 1) * 512], PS[d][:, :])
                    return ins
                b.op("dv", ev, rd=[rPS[0], rPS[1], rPS[2], rPS[3]], wr=[rSPILL])
                if dq == 0:
                    b.op("dv", lambda h: h.tensor_copy(ROW["t"][0:1, :], PS[4][0:1, :]),
                         rd=[rPS[4]], wr=[rROWS])

                    def r2c(h, l=l, fb=fb):
                        ins = None
                        for c4 in range(4):
                            ins = h.matmul(PS[5][:, c4:c4 + 1],
                                           ROW["t"][0:1, c4 * 128:(c4 + 1) * 128].bitcast(F32),
                                           ONER[0:1, 0:1].bitcast(F32),
                                           start=True, stop=True)
                        return ins
                    b.op("pe", r2c, rd=[rROWS], wr=[rPS[5]])
                    b.op("dv", lambda h, l=l, fb=fb: h.tensor_copy(
                        colap(f"beff{l}")[:, fb * 4:fb * 4 + 4], PS[5][:, 0:4]),
                        rd=[rPS[5]], wr=[rCOLS])
                for mi in range(4):
                    dma("gp",
                        wf_scr[l][fb * 4 + mi, :, dq * 4:(dq + 1) * 4, :],
                        SPILL[:, :].rearrange("p (k m c) -> p m k c",
                                              k=4, c=128)[:, mi],
                        rd=[rSPILL], wr=[rWF[l]])
            emit_cast()
        b.op("dv", lambda h, l=l: h.tensor_tensor(
            colap(f"beff{l}"), colap(f"beff{l}"), colap(f"entpb{l}"), ALU.add),
            rd=[rCOLS], wr=[rCOLS])
    while cast_jobs:
        emit_cast()

    b.barrier(exclude=(b.dma_sem_for(rWT),))

    NEG1_BF = ONES16[:, 1:2].bitcast(BF16)
    for i, nm in enumerate(("l3g0", "l3b0", "l3g1", "l3b1", "l4g0", "l4b0")):
        b.op("dv", lambda h, i=i, nm=nm: h.tensor_copy(
            GBF[:, i * 16:(i + 1) * 16], colap(nm)), rd=[rCOLS], wr=[rMISC])

    def colsums(w_ap, w_res, nuidx, g_rhs, b_rhs=None, foldcol=None):
        for m in range(16):
            wb_i = m % 4
            dma("sy", WS[wb_i][:, :, :], w_ap[m], rd=[w_res], wr=[rWS[wb_i]])

            def mkcs(h, wb_i=wb_i, m=m, g_rhs=g_rhs, b_rhs=b_rhs):
                ins = None
                for k in range(16):
                    ins = h.matmul(PS[6][:, m:m + 1], WS[wb_i][:, k, :],
                                   g_rhs(k), start=(k == 0), stop=(k == 15))
                if b_rhs is not None:
                    for k in range(16):
                        ins = h.matmul(PS[7][:, m:m + 1], WS[wb_i][:, k, :],
                                       b_rhs(k), start=(k == 0), stop=(k == 15))
                return ins
            wr_b = [rPS[6]] + ([rPS[7]] if b_rhs is not None else [])
            b.op("pe", mkcs, rd=[rWS[wb_i], rMISC], wr=wr_b)
        b.op("dv", lambda h, nuidx=nuidx: h.tensor_copy(
            NU1[:, nuidx * 16:nuidx * 16 + 16], PS[6][:, 0:16]),
            rd=[rPS[6]], wr=[rMISC])
        if g_rhs is not GONES:
            b.op("dv", lambda h, nuidx=nuidx: h.tensor_scalar(
                NU1[:, nuidx * 16:nuidx * 16 + 16],
                NU1[:, nuidx * 16:nuidx * 16 + 16], -1.0, None, ALU.mult),
                rd=[rMISC], wr=[rMISC])
        if foldcol is not None:
            b.op("dv", lambda h, foldcol=foldcol: h.tensor_tensor(
                colap(foldcol), colap(foldcol), PS[7][:, 0:16], ALU.add),
                rd=[rPS[7], rCOLS], wr=[rCOLS])

    GONES = lambda k: NEG1_BF
    colsums(wt_scr["w_amp"], rWT, 0, GONES)
    colsums(wt_scr["w_ph"], rWT, 1, GONES)
    colsums(we_scr[0], rWE[0], 2, lambda k: GBF[:, 0 * 16 + k:0 * 16 + k + 1],
            lambda k: GBF[:, 1 * 16 + k:1 * 16 + k + 1], "bpe0")
    colsums(we_scr[1], rWE[1], 3, lambda k: GBF[:, 2 * 16 + k:2 * 16 + k + 1],
            lambda k: GBF[:, 3 * 16 + k:3 * 16 + k + 1], "bpe1")
    colsums(wf_scr[1], rWF[1], 4, lambda k: GBF[:, 4 * 16 + k:4 * 16 + k + 1],
            lambda k: GBF[:, 5 * 16 + k:5 * 16 + k + 1], "beff1")
    for dst_, bsrc_ in (("bpe0", "l3b0"), ("bpe1", "l3b1"), ("beff1", "l4b0")):
        b.op("dv", lambda h, dst_=dst_, bsrc_=bsrc_: h.tensor_tensor(
            colap(dst_), colap(dst_), colap(bsrc_), ALU.add),
            rd=[rCOLS], wr=[rCOLS])

    mm_rot = [0]

    def dxd(w_ap, w_res, rhs_t, rhs_res, evac, extra_rhs_res=()):
        for m in range(16):
            wb_i = m % 4
            dma("sy", WS[wb_i][:, :, :], wsrc(w_ap, m),
                rd=[w_res] if w_res else [], wr=[rWS[wb_i]])
            bank = MM_BANKS[mm_rot[0] % 6]
            mm_rot[0] += 1

            is_h = int(rhs_t.shape[2]) == TT + 1

            def mk(h, wb_i=wb_i, bank=bank, rhs_t=rhs_t, is_h=is_h):
                ins = None
                for k in range(16):
                    rhs = (rhs_t[:, k, 1:TT + 1] if is_h else rhs_t[:, k, :])
                    ins = h.matmul(PS[bank][:, :], WS[wb_i][:, k, :],
                                   rhs,
                                   start=(k == 0), stop=(k == 15))
                return ins
            b.op("pe", mk, rd=[rWS[wb_i], rhs_res, *extra_rhs_res], wr=[rPS[bank]])
            evac(m, bank)

    def emit_ln_stats(src_t, src_res):
        b.op("ac", lambda h: h.activation(SQ[:, :, :], src_t[:, :, :], AF.Square),
             rd=[src_res], wr=[rSQ])

        def mk_sum(h):
            ins = None
            for c in range(16):
                ins = h.matmul(PS[6][0:1, :], ONESR[:, 0:1],
                               src_t[:, c, :],
                               start=(c == 0), stop=(c == 15))
            return ins
        b.op("pe", mk_sum, rd=[src_res, rMISC], wr=[rPS[6]])

        def mk_sq(h):
            ins = None
            for c in range(16):
                ins = h.matmul(PS[7][0:1, :], ONES_F16, SQ[:, c, :],
                               start=(c == 0), stop=(c == 15))
            return ins
        b.op("pe", mk_sq, rd=[rSQ, rMISC], wr=[rPS[7]])

        b.op("dv", lambda h: h.tensor_scalar(ROW["m"][0:1, :], PS[6][0:1, :],
                                             INV_D, None, ALU.mult),
             rd=[rPS[6]], wr=[rROWS])
        b.op("dv", lambda h: h.tensor_scalar(ROW["v"][0:1, :], PS[7][0:1, :],
                                             INV_D, None, ALU.mult),
             rd=[rPS[7]], wr=[rROWS])
        b.op("dv", lambda h: h.tensor_tensor(ROW["t"][0:1, :], ROW["m"][0:1, :],
                                             ROW["m"][0:1, :], ALU.mult),
             rd=[rROWS], wr=[rROWS])
        b.op("dv", lambda h: h.tensor_tensor(ROW["v"][0:1, :], ROW["v"][0:1, :],
                                             ROW["t"][0:1, :], ALU.subtract),
             rd=[rROWS], wr=[rROWS])
        b.op("ac", lambda h: h.activation(ROW["v"][0:1, :], ROW["v"][0:1, :],
                                          AF.Ln, bias=EPS_COL[0:1, :]),
             rd=[rROWS, rMISC], wr=[rROWS])
        b.op("ac", lambda h: h.activation(ROW["r"][0:1, :], ROW["v"][0:1, :],
                                          AF.Exp, scale=-0.5),
             rd=[rROWS], wr=[rROWS])
        b.op("dv", lambda h: h.tensor_tensor(ROW["m"][0:1, :], ROW["m"][0:1, :],
                                             ROW["r"][0:1, :], ALU.mult),
             rd=[rROWS], wr=[rROWS])
        b.op("pe", lambda h: h.matmul(PS[6][:, :], ONER[0:1, :],
                                      ROW["r"][0:1, :],
                                      start=True, stop=True),
             rd=[rROWS, rMISC], wr=[rPS[6]])
        b.op("pe", lambda h: h.matmul(PS[7][:, :], ONER[0:1, :],
                                      ROW["m"][0:1, :],
                                      start=True, stop=True),
             rd=[rROWS, rMISC], wr=[rPS[7]])

    def emit_ln(src_t, src_res, dst_fn, dst_res, g=None, bb=None,
                via_fn=None, via_res=None):
        emit_ln_stats(src_t, src_res)
        for c in range(16):
            d_ap = dst_fn(c)
            v_ap = via_fn(c) if via_fn is not None else d_ap
            v_res = via_res if via_fn is not None else dst_res
            b.op("dv", lambda h, c=c, v=v_ap: h.tensor_tensor(
                v, src_t[:, c, :], PS[6][:, :], ALU.mult),
                rd=[src_res, rPS[6]], wr=[v_res])
            if g is None:
                b.op("dv", lambda h, c=c, v=v_ap, d=d_ap: h.tensor_tensor(
                    d, v, PS[7][:, :], ALU.subtract),
                    rd=[rPS[7], v_res], wr=[dst_res])
            else:
                b.op("dv", lambda h, c=c, v=v_ap: h.tensor_tensor(
                    v, v, PS[7][:, :], ALU.subtract),
                    rd=[rPS[7], v_res], wr=[v_res])
                b.op("dv", lambda h, c=c, v=v_ap, d=d_ap, g=g, bb=bb: h.tensor_scalar(
                    d, v, colv(g, c), colv(bb, c), ALU.mult, ALU.add),
                    rd=[rCOLS, v_res], wr=[dst_res])

    def emit_mix(dst_t, dst_res, mu, om):
        for c in range(16):
            b.op("ac", lambda h, c=c, dst_t=dst_t: h.mul(
                dst_t[:, c, :], H[:, c, 0:TT], colv(om, c)),
                 rd=[rH, rCOLS], wr=[dst_res])
            b.op("dv", lambda h, c=c, dst_t=dst_t: h.scalar_tensor_tensor(
                dst_t[:, c, :], H[:, c, 1:TT + 1], colv(mu, c), dst_t[:, c, :],
                ALU.mult, ALU.add),
                rd=[rH, rCOLS, dst_res], wr=[dst_res])

    def t_in(tile):
        for tb in range(4):
            dma("gp", ST[:, :],
                x_in[tile * TT + tb * 128:tile * TT + (tb + 1) * 128, :].bitcast(F32R),
                rd=[rST], wr=[rST])
            for cg in range(4):
                bank = 6 + (cg % 2)

                def mk(h, cg=cg, bank=bank):
                    ins = None
                    for j in range(4):
                        c = cg * 4 + j
                        ins = h.transpose(
                            PS[bank][:, j * 128:(j + 1) * 128].bitcast(F32R),
                            ST[:, c * 128:(c + 1) * 128], IDENT[:, :])
                    return ins
                b.op("pe", mk, rd=[rST, rMISC], wr=[rPS[bank]])
                b.op("dv", lambda h, cg=cg, tb=tb, bank=bank: h.tensor_copy(
                    R[:, cg * 4:cg * 4 + 4, tb * 128:(tb + 1) * 128],
                    PS[bank][:, :].rearrange("p (a z) -> p a z", z=128)),
                    rd=[rPS[bank]], wr=[rR])

    def t_out(tile):
        for tb in range(4):
            for cg in range(4):
                bank = 6 + (cg % 2)

                def mk(h, cg=cg, tb=tb, bank=bank):
                    ins = None
                    for j in range(4):
                        c = cg * 4 + j
                        ins = h.transpose(
                            PS[bank][:, j * 128:(j + 1) * 128].bitcast(F32R),
                            R[:, c, tb * 128:(tb + 1) * 128], IDENT[:, :])
                    return ins
                b.op("pe", mk, rd=[rR, rMISC], wr=[rPS[bank]])
                b.op("dv", lambda h, cg=cg, bank=bank: h.tensor_copy(
                    ST[:, cg * 512:(cg + 1) * 512], PS[bank][:, :]),
                    rd=[rPS[bank]], wr=[rST])
            dma("gp", out[tile * TT + tb * 128:tile * TT + (tb + 1) * 128, :],
                ST[:, :].bitcast(F32), rd=[rST], wr=[rST])

    def carry_in(tile, ci):
        if tile % TILES_PER_SEQ == 0:
            b.op("dv", lambda h: h.tensor_scalar(
                H[:, :, 0], COLS[:, 0:16], 0.0, None, ALU.mult),
                rd=[rCOLS], wr=[rH])
        else:
            b.op("dv", lambda h, ci=ci: h.tensor_copy(H[:, :, 0], CARRY[ci][:, :]),
                 rd=[rCARRY], wr=[rH])

    def carry_save(ci):
        b.op("dv", lambda h, ci=ci: h.tensor_copy(CARRY[ci][:, :], H[:, :, TT]),
             rd=[rH], wr=[rCARRY])

    Hfull = H[:, :, 1:TT + 1]

    ffn_hooks = {}

    def emit_ffn():
        emit_ln_stats(R, rR)
        b.op("dv", lambda h: h.tensor_copy(RSTD[:, :], PS[6][:, :]),
             rd=[rPS[6]], wr=[rRSTD])
        b.op("dv", lambda h: h.tensor_copy(MRS[:, :], PS[7][:, :]),
             rd=[rPS[7]], wr=[rMRS])
        b.op("gp", lambda h: h.tensor_copy(SB1[:, :, :], R[:, :, :]),
             rd=[rR], wr=[rS1])

        def amp_evac(m, bank):
            b.op("dv", lambda h, m=m, bank=bank: h.tensor_tensor(
                S3[:, m, :], PS[bank][:, :], RSTD[:, :], ALU.mult),
                rd=[rPS[bank], rRSTD], wr=[rS3])
            b.op("dv", lambda h, m=m: h.scalar_tensor_tensor(
                S3[:, m, :], MRS[:, :], NU1[:, m:m + 1], S3[:, m, :],
                ALU.mult, ALU.add),
                rd=[rMRS, rMISC, rS3], wr=[rS3])
            b.op("ac", lambda h, m=m: h.activation(
                S3[:, m, :], S3[:, m, :], AF.Sigmoid, bias=colv("ampb", m)),
                rd=[rCOLS, rS3], wr=[rS3])

        def ph_evac(m, bank):
            b.op("dv", lambda h, m=m, bank=bank: h.tensor_tensor(
                H[:, m, 1:TT + 1], PS[bank][:, :], RSTD[:, :], ALU.mult),
                rd=[rPS[bank], rRSTD], wr=[rH])
            b.op("dv", lambda h, m=m: h.scalar_tensor_tensor(
                H[:, m, 1:TT + 1], MRS[:, :], NU1[:, 16 + m:17 + m],
                H[:, m, 1:TT + 1], ALU.mult, ALU.add),
                rd=[rMRS, rMISC, rH], wr=[rH])
            b.op("ac", lambda h, m=m: h.activation(
                H[:, m, 1:TT + 1], H[:, m, 1:TT + 1], AF.Tanh, bias=colv("phb", m)),
                rd=[rCOLS, rH], wr=[rH])

        dxd(wt_scr["w_amp"], rWT, SB1, rS1, amp_evac)
        if "amp" in ffn_hooks:
            ffn_hooks.pop("amp")()
            return
        dxd(wt_scr["w_ph"], rWT, SB1, rS1, ph_evac)
        if "ph" in ffn_hooks:
            ffn_hooks.pop("ph")()
            return
        b.op("dv", lambda h: h.tensor_scalar(S1[:, :, :], Hfull, 0.75, -2.0,
                                             ALU.is_gt, ALU.mult),
             rd=[rH], wr=[rS1])
        b.op("dv", lambda h: h.tensor_tensor(S1[:, :, :], S1[:, :, :], Hfull, ALU.add),
             rd=[rH, rS1], wr=[rS1])
        b.op("ac", lambda h: h.activation(S1[:, :, :], S1[:, :, :], AF.Sin,
                                          bias=PI4_COL, scale=PI),
             rd=[rS1, rMISC], wr=[rS1])
        b.op("dv", lambda h: h.scalar_tensor_tensor(
            AB[:, :, :], S1[:, :, :], SQRT2, S3[:, :, :], ALU.mult, ALU.mult),
            rd=[rS1, rS3], wr=[rA])
        if "h" in ffn_hooks:
            ffn_hooks.pop("h")()
            return
        def commute_ln(gname, ngname):
            emit_ln_stats(S1, rS1)
            b.op("dv", lambda h: h.tensor_copy(RSTD[:, :], PS[6][:, :]),
                 rd=[rPS[6]], wr=[rRSTD])
            b.op("dv", lambda h: h.tensor_copy(MRS[:, :], PS[7][:, :]),
                 rd=[rPS[7]], wr=[rMRS])
            for c in range(16):
                b.op("ac", lambda h, c=c, g=gname: h.mul(
                    AB[:, c, :], S1[:, c, :], colv(g, c)),
                    rd=[rS1, rCOLS], wr=[rA])
            for c in range(16):
                b.op("dv", lambda h, c=c, g=gname: h.scalar_tensor_tensor(
                    H[:, c, 1:TT + 1], S1[:, c, :], colv(g, c), RSTD[:, :],
                    ALU.mult, ALU.mult),
                    rd=[rS1, rRSTD, rCOLS], wr=[rH])
                b.op("dv", lambda h, c=c, ng=ngname: h.scalar_tensor_tensor(
                    H[:, c, 1:TT + 1], MRS[:, :], colv(ng, c),
                    H[:, c, 1:TT + 1], ALU.mult, ALU.add),
                    rd=[rMRS, rCOLS, rH], wr=[rH])

        def commuted_evac(nuidx, biascol):
            def ev(m, bank):
                b.op("dv", lambda h, m=m, bank=bank: h.tensor_tensor(
                    S1[:, m, :], PS[bank][:, :], RSTD[:, :], ALU.mult),
                    rd=[rPS[bank], rRSTD], wr=[rS1])
                b.op("dv", lambda h, m=m: h.scalar_tensor_tensor(
                    S1[:, m, :], MRS[:, :], NU1[:, nuidx * 16 + m:nuidx * 16 + m + 1],
                    S1[:, m, :], ALU.mult, ALU.add),
                    rd=[rMRS, rMISC, rS1], wr=[rS1])
                b.op("dv", lambda h, m=m: h.scalar_tensor_tensor(
                    S1[:, m, :], S1[:, m, :], colv(biascol, m),
                    H[:, m, 1:TT + 1], ALU.add, ALU.add),
                    rd=[rCOLS, rH, rS1], wr=[rS1])
            return ev

        dxd(wf_scr[0], rWF[0], AB, rA,
            lambda m, bank: b.op("dv", lambda h, m=m, bank=bank:
                h.scalar_tensor_tensor(S1[:, m, :], PS[bank][:, :],
                                       colv("beff0", m), AB[:, m, :],
                                       ALU.add, ALU.add),
                rd=[rPS[bank], rCOLS, rA], wr=[rS1]))
        commute_ln("l3g0", "ng30")
        dxd(we_scr[0], rWE[0], AB, rA, commuted_evac(2, "bpe0"))
        commute_ln("l4g0", "ng40")
        dxd(wf_scr[1], rWF[1], AB, rA, commuted_evac(4, "beff1"))
        commute_ln("l3g1", "ng31")
        dxd(we_scr[1], rWE[1], AB, rA, commuted_evac(3, "bpe1"))
        emit_ln(S1, rS1, lambda c: S3[:, c, :], rS3, g="l4g1", bb="l4b1")
        b.op("dv", lambda h: h.scalar_tensor_tensor(
            R[:, :, :], S3[:, :, :], 0.5, R[:, :, :], ALU.mult, ALU.add),
            rd=[rS3, rR], wr=[rR])

    def emit_attn(tile):
        carry_in(tile, 0)
        emit_ln(A, rA, lambda c: H[:, c, 1:TT + 1], rH, g="g1", bb="b1")
        emit_mix(SB1, rS1, "mu_r", "om_r")
        emit_mix(SB3, rS3, "mu_k", "om_k")
        dxd(wt_scr["w_rT"], rWT, SB1, rS1,
            lambda m, bank: b.op("ac", lambda h, m=m, bank=bank: h.activation(
                SQ[:, m, :], PS[bank][:, :], AF.Sigmoid),
                rd=[rPS[bank]], wr=[rSQ]))
        emit_mix(SB1, rS1, "mu_v", "om_v")
        carry_save(0)
        dxd(wt_scr["w_kT"], rWT, SB3, rS3,
            lambda m, bank: b.op("ac", lambda h, m=m, bank=bank: h.activation(
                H[:, m, 1:TT + 1], PS[bank][:, :], AF.Sigmoid, bias=colv("up", m)),
                rd=[rPS[bank], rCOLS], wr=[rH]))
        b.op("dv", lambda h: h.tensor_tensor(Hfull, Hfull, SQ[:, :, :], ALU.mult),
             rd=[rSQ, rH], wr=[rH])
        dxd(wt_scr["w_vT"], rWT, SB1, rS1,
            lambda m, bank: b.op("dv", lambda h, m=m, bank=bank: h.tensor_tensor(
                SB3[:, m, :], PS[bank][:, :], H[:, m, 1:TT + 1], ALU.mult),
                rd=[rPS[bank], rH], wr=[rS3]))
        dxd(wt_scr["w_oT"], rWT, SB3, rS3,
            lambda m, bank: b.op("dv", lambda h, m=m, bank=bank: h.tensor_tensor(
                A[:, m, :], A[:, m, :], PS[bank][:, :], ALU.add),
                rd=[rPS[bank]], wr=[rA]))
        carry_in(tile, 1)
        emit_ln(A, rA, lambda c: H[:, c, 1:TT + 1], rH, g="g2", bb="b2")
        emit_mix(SB1, rS1, "mu2_r", "om2_r")
        emit_mix(SB3, rS3, "mu2_k", "om2_k")
        dxd(wt_scr["w_crT"], rWT, SB1, rS1,
            lambda m, bank: b.op("ac", lambda h, m=m, bank=bank: h.activation(
                SQ[:, m, :], PS[bank][:, :], AF.Sigmoid),
                rd=[rPS[bank]], wr=[rSQ]))
        carry_save(1)

        def ck_evac(m, bank):
            b.op("dv", lambda h, m=m, bank=bank: h.tensor_scalar(
                HB[:, m, 1:TT + 1], PS[bank][:, :], 0.0, None, ALU.max),
                rd=[rPS[bank]], wr=[rH])
            b.op("ac", lambda h, m=m: h.activation(
                HB[:, m, 1:TT + 1], HB[:, m, 1:TT + 1], AF.Square),
                rd=[rH], wr=[rH])
        dxd(wt_scr["w_ckT"], rWT, SB3, rS3, ck_evac)
        dxd(wt_scr["w_cvT"], rWT, HB, rH,
            lambda m, bank: b.op("dv", lambda h, m=m, bank=bank: h.tensor_tensor(
                S1[:, m, :], PS[bank][:, :], SQ[:, m, :], ALU.mult),
                rd=[rPS[bank], rSQ], wr=[rS1]))
        b.op("dv", lambda h: h.tensor_tensor(A[:, :, :], A[:, :, :], S1[:, :, :],
                                             ALU.add),
             rd=[rS1], wr=[rA])
        b.op("gp", lambda h: h.tensor_tensor(R[:, :, :], R[:, :, :], A[:, :, :],
                                             ALU.add),
             rd=[rA, rR], wr=[rR])

    sheets = {"R": (R, rR), "A": (A, rA), "S1": (S1, rS1), "S3": (S3, rS3),
              "H": (H, rH), "SB1": (SB1, rS1), "AB": (AB, rA)}

    def dbg_dump(name):
        t_, r_ = sheets[name]
        nch = int(t_.shape[1])
        for c4 in range(nch // 4):
            dma("gp", out[c4 * 128:(c4 + 1) * 128, :].bitcast(t_.dtype),
                t_[:, c4 * 4:c4 * 4 + 4, 0:TT] if int(t_.shape[2]) == TT + 1
                else t_[:, c4 * 4:c4 * 4 + 4, :],
                rd=[r_], wr=[rMISC])

    stop = [False]

    def maybe_dump(stage):
        if DEBUG_DUMP and DEBUG_DUMP[1] == stage:
            dbg_dump(DEBUG_DUMP[0])
            stop[0] = True

    if DEBUG_DUMP and DEBUG_DUMP[1] in ("amp", "ph", "h"):
        def _hk():
            dbg_dump(DEBUG_DUMP[0])
            stop[0] = True
        ffn_hooks[DEBUG_DUMP[1]] = _hk

    for t in range(NTILES):
        if stop[0]:
            break
        t_in(t)
        maybe_dump("t_in")
        if stop[0]:
            break
        emit_ffn()
        maybe_dump("ffn1")
        if stop[0]:
            break
        emit_ln(R, rR, lambda c: A[:, c, :], rA)
        emit_attn(t)
        maybe_dump("attn")
        if stop[0]:
            break
        emit_ffn()
        maybe_dump("ffn2")
        if stop[0]:
            break
        t_out(t)

    final_tok = rST.w if not stop[0] else rMISC.w
    b.raw("gp", lambda h, t=final_tok: h.wait_ge(b.sem[t[0]], t[1]))

    import contextlib
    with contextlib.ExitStack() as ctx:
        for name in ("pe", "dv", "ac", "gp", "syb", *b.dma_sems):
            b.sem[name] = ctx.enter_context(nc.semaphore(name))
        block = ctx.enter_context(nc.Block())

        @block.tensor
        def _(e):
            for th in b.streams["pe"]:
                th(e)

        @block.vector
        def _(e):
            for th in b.streams["dv"]:
                th(e)

        @block.scalar
        def _(e):
            for th in b.streams["ac"]:
                th(e)

        @block.gpsimd
        def _(e):
            for th in b.streams["gp"]:
                th(e)

        @block.sync
        def _(e):
            for th in b.streams["sy"]:
                th(e)

    return nc


_NC_CACHE = None


def _col(v):
    return np.ascontiguousarray(np.asarray(v, dtype=np.float32).reshape(-1, 128).T)


def _tile_w(w):
    return np.ascontiguousarray(
        np.asarray(w, dtype=np.float32).reshape(16, 128, 16, 128).transpose(2, 1, 0, 3))


def kernel(**inp):
    global _NC_CACHE
    f32 = lambda a: np.ascontiguousarray(np.asarray(a, dtype=np.float32))
    T = lambda a: np.asarray(a, dtype=np.float32).T

    x = f32(inp["L0_attn_x"])

    common = {
        "w_amp": _tile_w(inp["L0_ffn_amp_w"]),
        "w_ph": _tile_w(inp["L0_ffn_ph_w"]),
        "w_rT": _tile_w(T(inp["L0_attn_tm_Wr"])),
        "w_kT": _tile_w(T(inp["L0_attn_tm_Wk"])),
        "w_vT": _tile_w(T(inp["L0_attn_tm_Wv"])),
        "w_oT": _tile_w(T(inp["L0_attn_tm_Wout"])),
        "w_crT": _tile_w(T(inp["L0_attn_cm_Wr"])),
        "w_ckT": _tile_w(T(inp["L0_attn_cm_Wk"])),
        "w_cvT": _tile_w(T(inp["L0_attn_cm_Wv"])),
        "ident": np.eye(128, dtype=np.float32),
        "ones_col": np.ones((128, 1), dtype=np.float32),
        "ones_row": np.ones((1, 128), dtype=np.float32),
        "c_mu_r": _col(inp["L0_attn_tm_mu_r"]),
        "c_mu_k": _col(inp["L0_attn_tm_mu_k"]),
        "c_mu_v": _col(inp["L0_attn_tm_mu_v"]),
        "c_mu2_r": _col(inp["L0_attn_cm_mu_r"]),
        "c_mu2_k": _col(inp["L0_attn_cm_mu_k"]),
        "c_u": _col(inp["L0_attn_tm_u"]),
        "c_g1": _col(inp["L0_attn_ln1_g"]),
        "c_b1": _col(inp["L0_attn_ln1_b"]),
        "c_g2": _col(inp["L0_attn_ln2_g"]),
        "c_b2": _col(inp["L0_attn_ln2_b"]),
        "c_ampb": _col(inp["L0_ffn_amp_b"]),
        "c_phb": _col(inp["L0_ffn_ph_b"]),
    }
    ent_w = np.asarray(inp["L0_ffn_ent_w"], dtype=np.float32)
    ent_pw = np.asarray(inp["L0_ffn_ent_pw"], dtype=np.float32)
    path_w = np.asarray(inp["L0_ffn_path_w"], dtype=np.float32)
    phase = np.asarray(inp["L0_ffn_phase"], dtype=np.float32)
    path_b = np.asarray(inp["L0_ffn_path_b"], dtype=np.float32)
    ent_b = np.asarray(inp["L0_ffn_ent_b"], dtype=np.float32)
    ent_pb = np.asarray(inp["L0_ffn_ent_pb"], dtype=np.float32)
    ln3_g = np.asarray(inp["L0_ffn_ln3_g"], dtype=np.float32)
    ln3_b = np.asarray(inp["L0_ffn_ln3_b"], dtype=np.float32)
    ln4_g = np.asarray(inp["L0_ffn_ln4_g"], dtype=np.float32)
    ln4_b = np.asarray(inp["L0_ffn_ln4_b"], dtype=np.float32)
    for l in range(2):
        common[f"w_entT{l}"] = np.ascontiguousarray(T(ent_w[l]))
        common[f"w_entp{l}"] = f32(ent_pw[l])
        common[f"c_entb{l}"] = _col(ent_b[l])
        common[f"c_entpb{l}"] = _col(ent_pb[l])
        common[f"c_l3g{l}"] = _col(ln3_g[l])
        common[f"c_l3b{l}"] = _col(ln3_b[l])
        common[f"c_l4g{l}"] = _col(ln4_g[l])
        common[f"c_l4b{l}"] = _col(ln4_b[l])
        for p in range(2):
            common[f"w_path{l}{p}"] = f32(path_w[l, p])
            common[f"r_phase{l}{p}"] = f32(phase[l, p])
            common[f"r_pb{l}{p}"] = f32(path_b[l, p])

    in_maps = []
    for c in range(8):
        m = dict(common)
        m["x"] = np.ascontiguousarray(x[2 * c:2 * c + 2].reshape(NROWS, D))
        in_maps.append(m)

    if _NC_CACHE is None:
        _NC_CACHE = build_program()
    res = run_bass_kernel_spmd(_NC_CACHE, in_maps, core_ids=list(range(8)))
    outs = [res.results[c]["out"].reshape(2, 2048, 2048) for c in range(8)]
    return np.concatenate(outs, axis=0)
